# version 17
# baseline (speedup 1.0000x reference)
import numpy as np
import ml_dtypes
import concourse.bass as bass
import concourse.mybir as mybir
from concourse.bass_utils import run_bass_kernel_spmd
from contextlib import ExitStack

F32 = mybir.dt.float32
BF16 = mybir.dt.bfloat16
AF = mybir.ActivationFunctionType
ALU = mybir.AluOpType
AX = mybir.AxisListType

D = 2048
T = 2048
NKD = 16
NCOL = 4
CW = 512
NW8 = 8
NSEQ = 2

LN_EPS = 1e-5
RMS_EPS = float(np.finfo(np.float32).eps)

DEBUG_TAPS = ()
STOP_AFTER = None


class _StopEmit(Exception):
    pass


class Ctr:
    def __init__(self, sem):
        self.sem = sem
        self.n = 0


def build_nc():
    nc = bass.Bass("TRN2", target_bir_lowering=False)

    xb_in = nc.dram_tensor("xb", [NSEQ, 128, NCOL, NKD, CW], F32, kind="ExternalInput").ap()
    wnames = ["wq", "wk", "wv", "wg", "wp", "wup", "wdn", "wex"]
    w_in = {n: nc.dram_tensor(n, [NW8, 128, NKD, 256], F32, kind="ExternalInput").ap()
            for n in wnames}
    wgate_in = nc.dram_tensor("wgate", [128, 128], F32, kind="ExternalInput").ap()
    masks_in = nc.dram_tensor("masks", [128, 2048], BF16, kind="ExternalInput").ap()
    ident_in = nc.dram_tensor("ident", [128, 128], BF16, kind="ExternalInput").ap()
    identf_in = nc.dram_tensor("identf", [128, 128], F32, kind="ExternalInput").ap()
    onesc_in = nc.dram_tensor("onesc", [128, 1], BF16, kind="ExternalInput").ap()
    onesr_in = nc.dram_tensor("onesr", [1, 128], F32, kind="ExternalInput").ap()
    out_t = nc.dram_tensor("outT", [NSEQ, D, T], F32, kind="ExternalOutput").ap()
    o2scr = nc.dram_tensor("o2scr", [128, NKD, T], BF16).ap()
    wkb = nc.dram_tensor("wkb", [NW8, 128, NKD, 256], BF16).ap()
    xnscr = nc.dram_tensor("xnscr", [128, NCOL, NKD, CW], BF16).ap()
    x2scr = nc.dram_tensor("x2scr", [128, NKD, T], F32).ap()
    rwscr = nc.dram_tensor("rwscr", [T], F32).ap()

    taps = {}
    def tapt(name, shape, dtype):
        if name in DEBUG_TAPS:
            taps[name] = nc.dram_tensor("tap_" + name, shape, dtype,
                                        kind="ExternalOutput").ap()
            return taps[name]
        return None

    tap_xhat = tapt("xhat", [128, NKD * T], BF16)
    tap_rows = tapt("rows", [2, T], F32)
    tap_q = tapt("q", [128, NKD * T], BF16)
    tap_k = tapt("k", [128, NKD * CW], BF16)
    tap_s = tapt("s", [128, 40 * CW], BF16)
    tap_vt = tapt("vt", [128, NKD * CW], BF16)
    tap_g = tapt("g", [128, 4 * T], BF16)
    tap_o2 = tapt("o2", [128, NKD * T], BF16)
    tap_x2 = tapt("x2", [128, NKD * T], F32)
    tap_xn2 = tapt("xn2", [128, NKD * T], BF16)
    tap_rws = tapt("rws", [T], F32)
    tap_up = tapt("up", [128, NKD * T], BF16)
    tap_small = tapt("small", [128, 512], F32)

    ctx = ExitStack()
    sb = lambda *a: ctx.enter_context(nc.sbuf_tensor(*a))
    psa = lambda *a: ctx.enter_context(nc.psum_tensor(*a))

    A = sb("A", [128, 32768], BF16)
    B = sb("B", [128, 32768], BF16)
    C = sb("C", [128, 20480], BF16)
    E = sb("E", [128, 8192], BF16)
    masksS = sb("masksS", [128, 2048], BF16)
    identS = sb("identS", [128, 128], BF16)
    identF = sb("identF", [128, 128], F32)
    onescS = sb("onescS", [128, 1], BF16)
    zeroS = sb("zeroS", [128, 1], BF16)
    wgateS = sb("wgateS", [128, 128], BF16)
    ROWS = sb("ROWS", [128, 4096], F32)
    SMALL = sb("SMALL", [128, 512], F32)

    PM = [psa(f"PM{i}", [128, 512], F32) for i in range(4)]
    PS0 = psa("PS0", [1, 512], F32)
    PS1 = psa("PS1", [1, 512], F32)
    PT = [psa(f"PT{i}", [128, 128], BF16) for i in range(2)]

    cG = Ctr(ctx.enter_context(nc.semaphore("sG")))
    cH = Ctr(ctx.enter_context(nc.semaphore("sH")))
    cPE = Ctr(ctx.enter_context(nc.semaphore("sPE")))
    cDV = Ctr(ctx.enter_context(nc.semaphore("sDV")))
    cAC = Ctr(ctx.enter_context(nc.semaphore("sAC")))
    cPL = Ctr(ctx.enter_context(nc.semaphore("sPL")))

    def chunk_col(arena, kd, tt):
        return arena[:, kd * T + tt * CW: kd * T + tt * CW + CW]

    def xhat_col(kd, tt):
        return A[:, tt * 8192 + kd * CW: tt * 8192 + kd * CW + CW]

    def A_kq_tile(oc):
        return A[:, oc * CW:(oc + 1) * CW]

    def A_xq(j):
        return A[:, 16384 + j * 8192: 16384 + (j + 1) * 8192]

    def A_wbuf(j):
        return A[:, 8192 + j * 4096: 8192 + (j + 1) * 4096]

    def B_vt(tc):
        return B[:, tc * CW:(tc + 1) * CW]

    def B_g(oc4):
        return B[:, 8192 + oc4 * T: 8192 + (oc4 + 1) * T]

    def B_wbuf(j):
        return B[:, 16384 + j * 4096: 16384 + (j + 1) * 4096]

    def B_rbc(tt):
        return B[:, 24576 + tt * 1024: 24576 + (tt + 1) * 1024].bitcast(F32)

    def B_eb(j):
        return B[:, 28672 + j * 512: 28672 + (j + 1) * 512]

    def B_eb2(j):
        return B[:, 30720 + j * 512: 30720 + (j + 1) * 512]

    def E_rbc(tt):
        return E[:, tt * 512:(tt + 1) * 512]

    def E_mbc(tt):
        return E[:, 2048 + tt * 512: 2048 + (tt + 1) * 512]

    def E_bc1(tt):
        j = tt % 2
        return E[:, j * 1024:(j + 1) * 1024].bitcast(F32)

    def E_bc2(tt):
        j = tt % 2
        return E[:, 2048 + j * 1024: 2048 + (j + 1) * 1024].bitcast(F32)

    def E_ev(j):
        return E[:, 4096 + j * 1024: 4096 + (j + 1) * 1024].bitcast(F32)

    def E_xr(j):
        return E[:, 4096 + j * 1024: 4096 + (j + 1) * 1024].bitcast(F32)

    def E_rwbc(tt):
        return E[:, tt * 1024:(tt + 1) * 1024].bitcast(F32)

    def C_wbuf(j):
        return C[:, j * 4096:(j + 1) * 4096]

    def C_rbc(tt):
        return C[:, 8192 + tt * 1024: 8192 + (tt + 1) * 1024].bitcast(F32)

    def C_eb(j):
        return C[:, 12288 + j * 512: 12288 + (j + 1) * 512]

    C_xr = C_rbc

    def C_ev(j):
        return C[:, 12288 + j * 1024: 12288 + (j + 1) * 1024].bitcast(F32)

    def C_eb5(j):
        return C[:, 16384 + j * 512: 16384 + (j + 1) * 512]

    def C_ev2(j):
        return C[:, 16384 + j * 1024: 16384 + (j + 1) * 1024].bitcast(F32)

    def C_stile(idx):
        return C[:, idx * CW:(idx + 1) * CW]

    def mask_ap(j):
        return masksS[:, j * CW:(j + 1) * CW]

    onesr = ROWS[0:1, 0:128]
    r0 = [ROWS[0:1, 128 + i * 512:128 + (i + 1) * 512] for i in range(3)]
    R_row = ROWS[0:1, 1664:3712]
    rw_row = R_row
    m_negb = ROWS[32:33, 0:1024].bitcast(BF16)
    m_negb64 = ROWS[64:65, 0:1024].bitcast(BF16)
    rs_store = {
        "wk": ROWS[32:33, 1024:2048].bitcast(BF16),
        "wup": ROWS[32:33, 2048:3072].bitcast(BF16),
        "wex": ROWS[32:33, 3072:4096].bitcast(BF16),
        "wq": ROWS[64:65, 1024:2048].bitcast(BF16),
        "wv": ROWS[64:65, 2048:3072].bitcast(BF16),
        "wg": ROWS[64:65, 3072:4096].bitcast(BF16),
    }
    rs_part = {"wk": 32, "wup": 32, "wex": 32, "wq": 64, "wv": 64, "wg": 64}

    rs_gate_b = SMALL[32:33, 448:452].bitcast(BF16)
    Ltok = SMALL[:, 0:64].bitcast(BF16)
    m1_s = SMALL[:, 64:80]
    m2_s = SMALL[:, 80:96]
    s_s = SMALL[:, 96:112]
    rinv_s = SMALL[:, 112:128]
    T1_s = SMALL[:, 128:256]
    T2_s = SMALL[:, 256:384]
    RWS_s = SMALL[:, 496:512]
    r2pg = SMALL[:, 480:496]
    Rpg = SMALL[:, 384:400]
    row16 = SMALL[0:16, 128:256]
    Lsb = SMALL[0:8, 128:384].bitcast(BF16)

    s_off = {}
    _n = 0
    for kc in range(16):
        for qq in range(kc // 4, 4):
            s_off[(kc, qq)] = _n
            _n += 1

    stop_box = [False]

    def check_stop(name):
        if STOP_AFTER is not None and STOP_AFTER == name:
            raise _StopEmit()

    def run_block(streams):
        with nc.Block(no_gpsimd_drain=True) as blk:
            for name in ["gpsimd", "tensor", "vector", "scalar", "sync"]:
                ops = streams.get(name)
                if ops:
                    def runner(eng, ops=ops):
                        for f in ops:
                            f(eng)
                    getattr(blk, name)(runner)

    def end_waits(streams):
        gw, hw = cG.n, cH.n
        streams.setdefault("gpsimd", []).append(lambda e, v=gw: e.wait_ge(cG.sem, v))
        streams.setdefault("sync", []).append(lambda e, v=hw: e.wait_ge(cH.sem, v))

    def tap_dma(streams, tap, src_ap, waits=()):
        if tap is None:
            return
        s_ops = streams.setdefault("sync", [])
        for c, v in waits:
            s_ops.append(lambda e, c=c, v=v: e.wait_ge(c.sem, v))
        def f(e):
            e.dma_start(tap, src_ap).then_inc(cH.sem, 16)
        s_ops.append(f)
        cH.n += 16

    pm_state = {}

    def pm_wait(t_ops, pid):
        if pid in pm_state:
            c, v = pm_state[pid]
            t_ops.append(lambda e, c=c, v=v: e.wait_ge(c.sem, v))

    rs_guard = [0]
    ln_ps_mark = [0]

    def prefetch_w(streams, wstate, w_hbm, w8_list, wbuf_fn, n, wait=None):
        g_ops = streams.setdefault("gpsimd", [])
        if wait is not None:
            c, v = wait
            g_ops.append(lambda e, c=c, v=v: e.wait_ge(c.sem, v))
        j0 = wstate["j"]
        for jj in range(n):
            j = j0 + jj
            if j >= 2 and (j - 2) in wstate["consumed"]:
                need = wstate["consumed"][j - 2]
                g_ops.append(lambda e, v=need: e.wait_ge(cPE.sem, v))
            dst = wbuf_fn(j % 2)[:].rearrange("p (k o) -> p k o", k=16)
            src = w_hbm[w8_list[jj]]
            def f(e, dst=dst, src=src):
                e.dma_start(dst, src).then_inc(cG.sem, 16)
            g_ops.append(f)
            cG.n += 16
            wstate["dma_done"][j] = cG.n
        wstate["issued"] = j0 + n

    def emit_proj(streams, w_hbm, w8_list, wbuf_fn, rhs_fn, cols, dest_fn,
                  act_func, rank1, rscale, rbc_fn, eb_fn,
                  wstate=None, eb_state=None, wqueue="gpsimd", wctr=None,
                  col_wait=None, rs_name=None, rs_reuse=False):
        g_ops = streams.setdefault(wqueue, [])
        t_ops = streams.setdefault("tensor", [])
        v_ops = streams.setdefault("vector", [])
        sc_ops = streams.setdefault("scalar", [])
        if wctr is None:
            wctr = cG
        col_seen = set()
        dest_marks = {}
        if rank1:
            rs_row = rs_store[rs_name]
            mrow = m_negb if rs_part[rs_name] == 32 else m_negb64

        if wstate is None:
            wstate = {"j": 0, "consumed": {}, "dma_done": {}}
        wstate.setdefault("issued", wstate["j"])
        if eb_state is None:
            eb_state = {"free": {}, "rot": 0}
        j0 = wstate["j"]
        consumed = wstate["consumed"]
        dma_done = wstate["dma_done"]
        eb_free = eb_state["free"]
        rot = [0]

        def wdma(j):
            if j >= 2:
                need = consumed[j - 2]
                g_ops.append(lambda e, v=need: e.wait_ge(cPE.sem, v))
            dst = wbuf_fn(j % 2)[:].rearrange("p (k o) -> p k o", k=16)
            src = w_hbm[w8_list[j - j0]]
            def f(e, dst=dst, src=src):
                e.dma_start(dst, src).then_inc(wctr.sem, 16)
            g_ops.append(f)
            wctr.n += 16
            dma_done[j] = wctr.n

        for j in range(j0, j0 + len(w8_list)):
            if j >= wstate["issued"]:
                wdma(j)
                wstate["issued"] = j + 1
            wv = wbuf_fn(j % 2)[:].rearrange("p (k o) -> p k o", k=16)
            t_ops.append(lambda e, v=dma_done[j], c=wctr: e.wait_ge(c.sem, v))
            w8v = w8_list[j - j0]
            if rank1 and not rs_reuse:
                if rs_guard[0]:
                    t_ops.append(lambda e, v=rs_guard[0]: e.wait_ge(cDV.sem, v))
                def rsmm(e, wv=wv):
                    ins = None
                    for kd in range(NKD):
                        ins = e.matmul(PS0[0:1, 0:256], onescS[:, 0:1], wv[:, kd, :],
                                       start=(kd == 0), stop=(kd == NKD - 1))
                    ins.then_inc(cPE.sem, 1)
                t_ops.append(rsmm)
                cPE.n += 1
                pe_rs = cPE.n
                dstslice = rs_row[0:1, w8v * 256:(w8v + 1) * 256]
                def rscp(e, dstslice=dstslice, v=pe_rs):
                    e.wait_ge(cPE.sem, v)
                    e.tensor_copy(dstslice, PS0[0:1, 0:256]).then_inc(cDV.sem, 1)
                v_ops.append(rscp)
                cDV.n += 1
                rs_guard[0] = cDV.n
                t_ops.append(lambda e, v=cDV.n: e.wait_ge(cDV.sem, v))

            for oc_sub in range(2):
                for tt in cols:
                    if col_wait is not None and tt not in col_seen:
                        col_seen.add(tt)
                        cw = col_wait[tt]
                        for c, v in (cw if isinstance(cw, list) else [cw]):
                            t_ops.append(lambda e, c=c, v=v: e.wait_ge(c.sem, v))
                    pm_i = rot[0]
                    rot[0] = (rot[0] + 1) % 3
                    pm_wait(t_ops, pm_i)
                    pm = PM[pm_i]
                    lhs = wv[:, :, oc_sub * 128:(oc_sub + 1) * 128]
                    def grp(e, pm=pm, lhs=lhs, tt=tt, w8v=w8v, oc_sub=oc_sub,
                            do_r1=rank1):
                        ins = None
                        for kd in range(NKD):
                            ins = e.matmul(pm[:], lhs[:, kd, :], rhs_fn(kd, tt),
                                           start=(kd == 0),
                                           stop=(kd == NKD - 1 and not do_r1))
                        if do_r1:
                            ins = e.matmul(
                                pm[:], rs_row[0:1, w8v * 256 + oc_sub * 128:
                                              w8v * 256 + (oc_sub + 1) * 128],
                                mrow[0:1, tt * CW:(tt + 1) * CW],
                                start=False, stop=True)
                        ins.then_inc(cPE.sem, 1)
                    t_ops.append(grp)
                    cPE.n += 1
                    pe_grp = cPE.n
                    dest = dest_fn(w8_list[j - j0], oc_sub, tt)
                    if rscale:
                        ebs = eb_state["rot"]
                        eb_state["rot"] = (ebs + 1) % 4
                        eb = eb_fn(ebs)
                        if ebs in eb_free:
                            v_ops.append(lambda e, v=eb_free[ebs]: e.wait_ge(cAC.sem, v))
                        def ev(e, pm=pm, eb=eb, tt=tt, v=pe_grp):
                            e.wait_ge(cPE.sem, v)
                            e.tensor_tensor(eb, pm[:], rbc_fn(tt), ALU.mult).then_inc(cDV.sem, 1)
                        v_ops.append(ev)
                        cDV.n += 1
                        pm_state[pm_i] = (cDV, cDV.n)
                        dv_v = cDV.n
                        def ac(e, eb=eb, dest=dest, v=dv_v, fn=act_func):
                            e.wait_ge(cDV.sem, v)
                            e.activation(dest, eb, fn).then_inc(cAC.sem, 1)
                        sc_ops.append(ac)
                        cAC.n += 1
                        eb_free[ebs] = cAC.n
                        dest_marks[(w8v, oc_sub)] = cAC.n
                    else:
                        def ac2(e, pm=pm, dest=dest, v=pe_grp, fn=act_func):
                            e.wait_ge(cPE.sem, v)
                            if fn is None:
                                e.copy(dest, pm[:]).then_inc(cAC.sem, 1)
                            else:
                                e.activation(dest, pm[:], fn).then_inc(cAC.sem, 1)
                        sc_ops.append(ac2)
                        cAC.n += 1
                        pm_state[pm_i] = (cAC, cAC.n)
                        dest_marks[(w8v, oc_sub)] = cAC.n
            consumed[j] = cPE.n
        wstate["j"] = j0 + len(w8_list)
        return dest_marks

    def build_rbc(streams, rbc_fn, src_row, cols=range(NCOL)):
        t_ops = streams.setdefault("tensor", [])
        v_ops = streams.setdefault("vector", [])
        for tt in cols:
            pm_wait(t_ops, 3)
            def bmm(e, tt=tt):
                e.matmul(PM[3][:], onesr, src_row[0:1, tt * CW:(tt + 1) * CW],
                         start=True, stop=True).then_inc(cPE.sem, 1)
            t_ops.append(bmm)
            cPE.n += 1
            v = cPE.n
            def cp(e, tt=tt, v=v):
                e.wait_ge(cPE.sem, v)
                e.tensor_copy(rbc_fn(tt), PM[3][:]).then_inc(cDV.sem, 1)
            v_ops.append(cp)
            cDV.n += 1
            pm_state[3] = (cDV, cDV.n)
            t_ops.append(lambda e, v=cDV.n: e.wait_ge(cDV.sem, v))

    def emit_ln_stats(streams, src_fn, eb_fn, finish_fn, col_wait=None):
        sc_ops = streams.setdefault("scalar", [])
        t_ops = streams.setdefault("tensor", [])
        g_ops = streams.setdefault("gpsimd", [])
        if col_wait is None:
            g_ops.append(lambda e, v=cAC.n: e.wait_ge(cAC.sem, v))
        slot_reader = {}
        for tt in range(NCOL):
            if col_wait is not None:
                sc_ops.append(lambda e, v=col_wait[tt]: e.wait_ge(cG.sem, v))
                t_ops.append(lambda e, v=col_wait[tt]: e.wait_ge(cG.sem, v))
                g_ops.append(lambda e, v=col_wait[tt]: e.wait_ge(cG.sem, v))
            if tt > 0:
                t_ops.append(lambda e, v=ln_ps_mark[0]: e.wait_ge(cDV.sem, v))
            for kd in range(NKD):
                ebs = kd % 4
                if kd % 2 == 0:
                    if ebs in slot_reader:
                        sc_ops.append(lambda e, v=slot_reader[ebs]:
                                      e.wait_ge(cPE.sem, v))
                    def sq(e, kd=kd, tt=tt, ebs=ebs):
                        e.activation(eb_fn(ebs), src_fn(kd, tt),
                                     AF.Square).then_inc(cAC.sem, 1)
                    sc_ops.append(sq)
                    cAC.n += 1
                    prod = (cAC, cAC.n)
                else:
                    if ebs in slot_reader:
                        g_ops.append(lambda e, v=slot_reader[ebs]:
                                     e.wait_ge(cPE.sem, v))
                    def sqp(e, kd=kd, tt=tt, ebs=ebs):
                        e.tensor_tensor(eb_fn(ebs), src_fn(kd, tt),
                                        src_fn(kd, tt),
                                        ALU.mult).then_inc(cPL.sem, 1)
                    g_ops.append(sqp)
                    cPL.n += 1
                    prod = (cPL, cPL.n)
                def mm2(e, kd=kd, tt=tt, ebs=ebs, c=prod[0], v=prod[1]):
                    e.matmul(PS0[0:1, :], onescS[:, 0:1], src_fn(kd, tt),
                             start=(kd == 0), stop=(kd == NKD - 1))
                    e.wait_ge(c.sem, v)
                    e.matmul(PS1[0:1, :], onescS[:, 0:1], eb_fn(ebs),
                             start=(kd == 0), stop=(kd == NKD - 1)).then_inc(cPE.sem, 1)
                t_ops.append(mm2)
                cPE.n += 1
                slot_reader[ebs] = cPE.n
            finish_fn(tt)
            rs_guard[0] = max(rs_guard[0], cDV.n)

    cur = {}

    def ln1_finish(tt):
        v_ops = cur.setdefault("vector", [])
        sc_ops = cur.setdefault("scalar", [])
        pe_v = cPE.n
        def rows1(e, tt=tt, v=pe_v):
            e.wait_ge(cPE.sem, v)
            e.tensor_scalar_mul(r0[0], PS0[0:1, :], 1.0 / D)
            e.tensor_tensor(r0[1], r0[0], r0[0], ALU.mult)
            e.tensor_scalar_mul(r0[2], PS1[0:1, :], 1.0 / D)
            e.tensor_tensor(r0[2], r0[2], r0[1], ALU.subtract)
            e.tensor_scalar_mul(r0[1], r0[2], 1.0 + RMS_EPS)
            e.tensor_scalar_add(r0[1], r0[1],
                                RMS_EPS * LN_EPS).then_inc(cDV.sem, 1)
        v_ops.append(rows1)
        cDV.n += 1
        ln_ps_mark[0] = cDV.n
        dv_v = cDV.n
        def rows1b(e, tt=tt, v=dv_v):
            e.wait_ge(cDV.sem, v)
            e.activation(r0[1], r0[1], AF.Ln)
            e.activation(R_row[0:1, tt * CW:(tt + 1) * CW], r0[1],
                         AF.Exp, scale=-0.5).then_inc(cAC.sem, 1)
        sc_ops.append(rows1b)
        cAC.n += 1
        ac_v = cAC.n
        t_ops = cur.setdefault("tensor", [])
        s_ops = cur.setdefault("sync", [])
        pm_wait(t_ops, 3)
        pm_wait(t_ops, 2)
        def bc1(e, tt=tt, v=cDV.n, va=ac_v):
            e.wait_ge(cDV.sem, v)
            e.matmul(PM[3][:], onesr, r0[0], start=True,
                     stop=True).then_inc(cPE.sem, 1)
            e.wait_ge(cAC.sem, va)
            e.matmul(PM[2][:], onesr, R_row[0:1, tt * CW:(tt + 1) * CW],
                     start=True, stop=True).then_inc(cPE.sem, 1)
        t_ops.append(bc1)
        cPE.n += 2
        def bc1c(e, tt=tt, v=cPE.n):
            e.wait_ge(cPE.sem, v)
            e.tensor_copy(E_mbc(tt), PM[3][:])
            e.tensor_copy(E_rbc(tt), PM[2][:]).then_inc(cDV.sem, 1)
        v_ops.append(bc1c)
        cDV.n += 1
        pm_state[3] = (cDV, cDV.n)
        pm_state[2] = (cDV, cDV.n)
        t_ops.append(lambda e, v=cDV.n: e.wait_ge(cDV.sem, v))
        g_ops2 = cur.setdefault("gpsimd", [])
        bc_ready = cDV.n
        for kd in range(10):
            last = (kd == 9)
            def xna(e, kd=kd, tt=tt, last=last):
                xt = xhat_col(kd, tt)
                e.tensor_tensor(xt, xt, E_mbc(tt), ALU.subtract)
                ins = e.tensor_tensor(xt, xt, E_rbc(tt), ALU.mult)
                if last:
                    ins.then_inc(cDV.sem, 1)
            v_ops.append(xna)
        cDV.n += 1
        g_ops2.append(lambda e, v=bc_ready: e.wait_ge(cDV.sem, v))
        for kd in range(10, NKD):
            last = (kd == NKD - 1)
            def xnap(e, kd=kd, tt=tt, last=last):
                xt = xhat_col(kd, tt)
                e.tensor_tensor(xt, xt, E_mbc(tt), ALU.subtract)
                ins = e.tensor_tensor(xt, xt, E_rbc(tt), ALU.mult)
                if last:
                    ins.then_inc(cPL.sem, 1)
            g_ops2.append(xnap)
        cPL.n += 1
        ln1col[tt] = [(cDV, cDV.n), (cPL, cPL.n)]
        def xst(e, tt=tt, v=cDV.n, vp=cPL.n):
            e.wait_ge(cDV.sem, v)
            e.wait_ge(cPL.sem, vp)
            e.dma_start(xnscr[:, tt],
                        A[:, tt * 8192:(tt + 1) * 8192]
                        .rearrange("p (k t) -> p k t", k=NKD)).then_inc(cH.sem, 16)
        s_ops.append(xst)
        cH.n += 16
        xnst_tgt[tt] = cH.n

    def ln2_finish(tt):
        v_ops = cur.setdefault("vector", [])
        sc_ops = cur.setdefault("scalar", [])
        t_ops = cur.setdefault("tensor", [])
        pe_v = cPE.n
        def rows2(e, tt=tt, v=pe_v):
            e.wait_ge(cPE.sem, v)
            e.tensor_scalar_mul(r0[0], PS0[0:1, :], 1.0 / D)
            e.tensor_tensor(r0[1], r0[0], r0[0], ALU.mult)
            e.tensor_scalar_mul(r0[2], PS1[0:1, :], 1.0 / D)
            e.tensor_tensor(r0[2], r0[2], r0[1], ALU.subtract)
            e.tensor_scalar_add(r0[1], r0[2], LN_EPS).then_inc(cDV.sem, 1)
        v_ops.append(rows2)
        cDV.n += 1
        ln_ps_mark[0] = cDV.n
        dv_v = cDV.n
        def rows2b(e, v=dv_v):
            e.wait_ge(cDV.sem, v)
            e.activation(r0[1], r0[1], AF.Ln)
            e.activation(r0[1], r0[1], AF.Exp, scale=-0.5).then_inc(cAC.sem, 1)
        sc_ops.append(rows2b)
        cAC.n += 1
        ac_v = cAC.n
        pm_wait(t_ops, 3)
        pm_wait(t_ops, 2)
        def bco(e, va=ac_v):
            e.wait_ge(cAC.sem, va)
            e.matmul(PM[3][:], onesr, r0[0], start=True,
                     stop=True).then_inc(cPE.sem, 1)
            e.matmul(PM[2][:], onesr, r0[1], start=True,
                     stop=True).then_inc(cPE.sem, 1)
        t_ops.append(bco)
        cPE.n += 2
        def cpo(e, tt=tt, v=cPE.n):
            e.wait_ge(cPE.sem, v)
            e.tensor_copy(E_mbc(tt), PM[3][:])
            e.tensor_copy(E_rbc(tt), PM[2][:]).then_inc(cDV.sem, 1)
        v_ops.append(cpo)
        cDV.n += 1
        pm_state[3] = (cDV, cDV.n)
        pm_state[2] = (cDV, cDV.n)
        t_ops.append(lambda e, v=cDV.n: e.wait_ge(cDV.sem, v))
        g_ops2 = cur.setdefault("gpsimd", [])
        bc_ready = cDV.n
        for kd in range(10):
            last = (kd == 9)
            def xna2(e, kd=kd, tt=tt, last=last):
                xt = chunk_col(B, kd, tt)
                e.tensor_tensor(xt, xt, E_mbc(tt), ALU.subtract)
                ins = e.tensor_tensor(xt, xt, E_rbc(tt), ALU.mult)
                if last:
                    ins.then_inc(cDV.sem, 1)
            v_ops.append(xna2)
        cDV.n += 1
        g_ops2.append(lambda e, v=bc_ready: e.wait_ge(cDV.sem, v))
        for kd in range(10, NKD):
            last = (kd == NKD - 1)
            def xna2p(e, kd=kd, tt=tt, last=last):
                xt = chunk_col(B, kd, tt)
                e.tensor_tensor(xt, xt, E_mbc(tt), ALU.subtract)
                ins = e.tensor_tensor(xt, xt, E_rbc(tt), ALU.mult)
                if last:
                    ins.then_inc(cPL.sem, 1)
            g_ops2.append(xna2p)
        cPL.n += 1
        ln2col[tt] = [(cDV, cDV.n), (cPL, cPL.n)]

    cur = {}
    def consts(e):
        e.dma_start(masksS[:], masks_in).then_inc(cH.sem, 16)
        e.dma_start(identS[:], ident_in).then_inc(cH.sem, 16)
        e.dma_start(identF[:], identf_in).then_inc(cH.sem, 16)
        e.dma_start(onescS[:], onesc_in).then_inc(cH.sem, 16)
        e.dma_start(onesr, onesr_in).then_inc(cH.sem, 16)
    cur["sync"] = [consts]
    cH.n += 80
    def wg_load(e):
        e.memset(zeroS[:], 0.0)
        e.dma_start(wgateS[:], wgate_in).then_inc(cG.sem, 16)
    cur["gpsimd"] = [wg_load]
    cG.n += 16
    end_waits(cur)
    run_block(cur)

    wkb_done = [0]
    try:
     for s in range(NSEQ):
        cur = {}
        xh_tgt = {}
        if s > 0:
            def mreload(e):
                e.dma_start(masksS[:], masks_in).then_inc(cH.sem, 16)
            cur.setdefault("sync", []).append(mreload)
            cH.n += 16
        g_ops = cur.setdefault("gpsimd", [])
        for tt in range(NCOL):
            def xhat_load(e, s=s, tt=tt):
                e.dma_start(A[:, tt * 8192:(tt + 1) * 8192]
                            .rearrange("p (k t) -> p k t", k=NKD),
                            xb_in[s][:, tt]).then_inc(cG.sem, 16)
            g_ops.append(xhat_load)
            cG.n += 16
            xh_tgt[tt] = cG.n
        ws_q = {"j": 0, "consumed": {}, "dma_done": {}}
        prefetch_w(cur, ws_q, w_in["wq"], [0, 1], C_wbuf, 2)
        if s == 0:
            def wk_cast(e):
                for w8 in range(NW8):
                    e.dma_start(wkb[w8], w_in["wk"][w8]).then_inc(cG.sem, 16)
            cur.setdefault("gpsimd", []).append(wk_cast)
            cG.n += 16 * NW8
            wkb_done[0] = cG.n
        ln1col = {}
        xnst_tgt = {}
        emit_ln_stats(cur, xhat_col, C_eb, ln1_finish, col_wait=xh_tgt)
        if s == 0:
            tap_dma(cur, tap_xhat, A[:, 0:NKD * T], [(cDV, cDV.n)])
        if tap_rows is not None and s == 0:
            def trows(e, v=cDV.n):
                e.wait_ge(cDV.sem, v)
                e.dma_start(tap_rows[1:2, :], R_row).then_inc(cH.sem, 16)
            cur.setdefault("sync", []).append(trows)
            cH.n += 16
        emit_proj(cur, w_in["wq"], list(range(NW8)), C_wbuf,
                  xhat_col, list(range(NCOL)),
                  lambda w8, ocs, tt: chunk_col(B, w8 * 2 + ocs, tt),
                  AF.Silu, rank1=False, rscale=False, rbc_fn=None, eb_fn=None,
                  wstate=ws_q, col_wait=ln1col)
        pq_pe_end = cPE.n
        if s == 0:
            tap_dma(cur, tap_q, B[:, 0:NKD * T], [(cAC, cAC.n)])

        ws_k = {"j": 0, "consumed": {}, "dma_done": {}}
        eb_k = {"free": {}, "rot": 0}
        if s == 0:
            cur.setdefault("sync", []).append(
                lambda e, v=wkb_done[0]: e.wait_ge(cG.sem, v))
        cur.setdefault("sync", []).append(
            lambda e, v=pq_pe_end: e.wait_ge(cPE.sem, v))
        xq_guard = {}
        xq_tgt = {}
        def emit_xq(kq):
            g_ops = cur.setdefault("gpsimd", [])
            if kq >= 2:
                need = xq_guard[kq - 2]
                g_ops.append(lambda e, v=need: e.wait_ge(cPE.sem, v))
            else:
                g_ops.append(lambda e, v=pq_pe_end: e.wait_ge(cPE.sem, v))
            def xq_load(e, kq=kq, v=xnst_tgt[kq]):
                e.wait_ge(cH.sem, v)
                e.dma_start(A_xq(kq % 2).rearrange("p (k t) -> p k t", k=NKD),
                            xnscr[:, kq]).then_inc(cG.sem, 16)
            g_ops.append(xq_load)
            cG.n += 16
            xq_tgt[kq] = cG.n
        emit_xq(0)
        emit_xq(1)
        for kq in range(NCOL):
            cur.setdefault("tensor", []).append(
                lambda e, v=xq_tgt[kq]: e.wait_ge(cG.sem, v))
            xqv = A_xq(kq % 2).rearrange("p (k t) -> p k t", k=NKD)
            emit_proj(cur, wkb, list(range(NW8)), A_wbuf,
                      lambda kd, tt, xqv=xqv: xqv[:, kd, :], [kq],
                      lambda w8, ocs, tt: A_kq_tile(w8 * 2 + ocs),
                      AF.Silu, rank1=False, rscale=False, rbc_fn=None, eb_fn=None,
                      wstate=ws_k, eb_state=eb_k, wqueue="sync", wctr=cH)
            xq_guard[kq] = cPE.n
            if kq + 2 < NCOL:
                emit_xq(kq + 2)
            k_done = cAC.n
            if kq == 0 and tap_k is not None and s == 0:
                tap_dma(cur, tap_k, A[:, 0:NKD * CW], [(cAC, k_done)])

            t_ops = cur.setdefault("tensor", [])
            v_ops = cur.setdefault("vector", [])
            sc_ops = cur.setdefault("scalar", [])
            t_ops.append(lambda e, v=k_done: e.wait_ge(cAC.sem, v))
            roti = [0]
            for kcl in range(4):
                kc = kq * 4 + kcl
                for qq in range(kq, 4):
                    pm_i = roti[0]
                    roti[0] = (roti[0] + 1) % 3
                    pm_wait(t_ops, pm_i)
                    pm = PM[pm_i]
                    def smm(e, pm=pm, kcl=kcl, qq=qq):
                        ins = None
                        for od in range(NKD):
                            ins = e.matmul(pm[:],
                                           A_kq_tile(od)[:, kcl * 128:(kcl + 1) * 128],
                                           chunk_col(B, od, qq),
                                           start=(od == 0), stop=(od == NKD - 1))
                        ins.then_inc(cPE.sem, 1)
                    t_ops.append(smm)
                    cPE.n += 1
                    pev = cPE.n
                    dst = C_stile(s_off[(kc, qq)])
                    if qq == kq:
                        def sev(e, pm=pm, dst=dst, kcl=kcl, v=pev):
                            e.wait_ge(cPE.sem, v)
                            e.tensor_tensor(dst, pm[:], mask_ap(kcl), ALU.mult).then_inc(cDV.sem, 1)
                        v_ops.append(sev)
                        cDV.n += 1
                        pm_state[pm_i] = (cDV, cDV.n)
                    else:
                        def sev2(e, pm=pm, dst=dst, v=pev):
                            e.wait_ge(cPE.sem, v)
                            e.copy(dst, pm[:]).then_inc(cAC.sem, 1)
                        sc_ops.append(sev2)
                        cAC.n += 1
                        pm_state[pm_i] = (cAC, cAC.n)
            xq_guard[kq] = cPE.n
        end_waits(cur)
        run_block(cur)
        check_stop('PKS')

        if tap_s is not None and s == 0:
            cur = {}
            tap_dma(cur, tap_s, C[:, 0:40 * CW])
            end_waits(cur)
            run_block(cur)

        ws_vg = {"j": 0, "consumed": {}, "dma_done": {}}
        eb_vg = {"free": {}, "rot": 0}
        o2t_box = [0]
        xh2_tgt = {}
        cur = {}
        eb2_free = {}
        po_prev = [0]
        pvt_prev = [0]
        for eq in range(4):
            if eq == 0:
                s_ops0 = cur.setdefault("sync", [])
                xh2_tgt[2] = (cH, 0)
                xh2_tgt[3] = (cH, 0)
                for tt in range(2):
                    def xhat_load2(e, tt=tt, v=xnst_tgt[tt]):
                        e.wait_ge(cH.sem, v)
                        e.dma_start(A[:, tt * 8192:(tt + 1) * 8192]
                                    .rearrange("p (k t) -> p k t", k=NKD),
                                    xnscr[:, tt]).then_inc(cH.sem, 16)
                    s_ops0.append(xhat_load2)
                    cH.n += 16
                    xh2_tgt[tt] = (cH, cH.n)
            t_ops = cur.setdefault("tensor", [])
            v_ops = cur.setdefault("vector", [])
            sc_ops = cur.setdefault("scalar", [])
            s_ops = cur.setdefault("sync", [])
            if eq > 0:
                sc_ops.append(lambda e, v=po_prev[0]: e.wait_ge(cPE.sem, v))
            if s == 0 and eq == 0:
                pass
            roti_v = [0]
            for w8l in range(2):
                w8 = 2 * eq + w8l
                jv = ws_vg["j"]
                if jv >= 2:
                    need = ws_vg["consumed"][jv - 2]
                    cur.setdefault("gpsimd", []).append(
                        lambda e, v=need: e.wait_ge(cPE.sem, v))
                dst = B_wbuf(jv % 2)[:].rearrange("p (k o) -> p k o", k=16)
                def fw(e, dst=dst, w8=w8):
                    e.dma_start(dst, w_in["wv"][w8]).then_inc(cG.sem, 16)
                cur.setdefault("gpsimd", []).append(fw)
                cG.n += 16
                ws_vg["dma_done"][jv] = cG.n
                wv_view = B_wbuf(jv % 2)[:].rearrange("p (k o) -> p k o", k=16)
                t_ops.append(lambda e, v=cG.n: e.wait_ge(cG.sem, v))
                for tc in range(16):
                    if eq == 0 and w8l == 0:
                        c, v = xh2_tgt[tc // 4]
                        t_ops.append(lambda e, c=c, v=v: e.wait_ge(c.sem, v))
                    pm_i = roti_v[0]
                    roti_v[0] = (roti_v[0] + 1) % 3
                    pm_wait(t_ops, pm_i)
                    pm = PM[pm_i]
                    xsl = A[:, (tc // 4) * 8192 + (tc % 4) * 128:
                            (tc // 4) * 8192 + (tc % 4) * 128 + 1]
                    def vgrp(e, pm=pm, tc=tc, wv_view=wv_view):
                        ins = None
                        for kd in range(NKD):
                            xs = A[:, (tc // 4) * 8192 + kd * CW + (tc % 4) * 128:
                                   (tc // 4) * 8192 + kd * CW + (tc % 4) * 128 + 128]
                            ins = e.matmul(pm[:, 0:256], xs, wv_view[:, kd, :],
                                           start=(kd == 0), stop=(kd == NKD - 1))
                        ins.then_inc(cPE.sem, 1)
                    t_ops.append(vgrp)
                    cPE.n += 1
                    pev = cPE.n
                    def vev(e, pm=pm, tc=tc, w8l=w8l, v=pev):
                        e.wait_ge(cPE.sem, v)
                        e.activation(B_vt(tc)[:, w8l * 256:(w8l + 1) * 256],
                                     pm[:, 0:256],
                                     AF.Silu).then_inc(cAC.sem, 1)
                    sc_ops.append(vev)
                    cAC.n += 1
                    pm_state[pm_i] = (cAC, cAC.n)
                ws_vg["consumed"][jv] = cPE.n
                ws_vg["j"] = jv + 1
                ws_vg["issued"] = jv + 1
            pvt_done = cAC.n
            pvt_prev[0] = pvt_done
            if eq == 0 and s == 0:
                tap_dma(cur, tap_vt, B[:, 0:NKD * CW], [(cH, pvt_done)])
            def g_dest(w8, ocs, tt, eq=eq):
                oc4 = (w8 - 2 * eq) * 2 + ocs
                return B_g(oc4)[:, tt * CW:(tt + 1) * CW]
            emit_proj(cur, w_in["wg"], [2 * eq, 2 * eq + 1], B_wbuf,
                      xhat_col, list(range(NCOL)),
                      g_dest, AF.Sigmoid, rank1=False, rscale=False,
                      rbc_fn=None, eb_fn=None, wstate=ws_vg, eb_state=eb_vg)
            pg_done = cAC.n
            pg_pe_done = cPE.n
            if eq == 0 and s == 0:
                tap_dma(cur, tap_g, B[:, 8192:8192 + 4 * T], [(cAC, pg_done)])
            if eq == 3:
                def o2_load(e, v=pg_pe_done):
                    e.wait_ge(cPE.sem, v)
                    e.dma_start(A[:, 0:24576].rearrange("p (k t) -> p k t", k=12),
                                o2scr[:, 0:12, :]).then_inc(cH.sem, 16)
                s_ops.append(o2_load)
                cH.n += 16
            t_ops = cur.setdefault("tensor", [])
            v_ops = cur.setdefault("vector", [])
            t_ops.append(lambda e, v=pvt_done: e.wait_ge(cAC.sem, v))
            v_ops.append(lambda e, v=pg_done: e.wait_ge(cAC.sem, v))
            roti = [0]
            eb_free = eb2_free
            for qq in range(NCOL):
                for oc4 in range(4):
                    pm_i = roti[0]
                    roti[0] = (roti[0] + 1) % 3
                    pm_wait(t_ops, pm_i)
                    pm = PM[pm_i]
                    nkc = 4 * qq + 4
                    def omm(e, pm=pm, qq=qq, oc4=oc4, nkc=nkc):
                        ins = None
                        for kc in range(nkc):
                            ins = e.matmul(pm[:], B_vt(kc)[:, oc4 * 128:(oc4 + 1) * 128],
                                           C_stile(s_off[(kc, qq)]),
                                           start=(kc == 0), stop=(kc == nkc - 1))
                        ins.then_inc(cPE.sem, 1)
                    t_ops.append(omm)
                    cPE.n += 1
                    pev = cPE.n
                    ebs = (qq * 4 + oc4) % 4
                    if ebs in eb_free:
                        v_ops.append(lambda e, v=eb_free[ebs]: e.wait_ge(cH.sem, v))
                    def oev(e, pm=pm, qq=qq, oc4=oc4, ebs=ebs, v=pev):
                        e.wait_ge(cPE.sem, v)
                        e.tensor_tensor(B_eb2(ebs), pm[:],
                                        B_g(oc4)[:, qq * CW:(qq + 1) * CW],
                                        ALU.mult).then_inc(cDV.sem, 1)
                    v_ops.append(oev)
                    cDV.n += 1
                    pm_state[pm_i] = (cDV, cDV.n)
                    dvv = cDV.n
                    oc = eq * 4 + oc4
                    def ost(e, ebs=ebs, oc=oc, qq=qq, v=dvv):
                        e.wait_ge(cDV.sem, v)
                        e.dma_start(o2scr[:, oc, qq * CW:(qq + 1) * CW],
                                    B_eb2(ebs)).then_inc(cH.sem, 16)
                    s_ops.append(ost)
                    cH.n += 16
                    eb_free[ebs] = cH.n
            po_prev[0] = cPE.n
            if eq == 3:
                def o2_load2(e, v=cH.n):
                    e.wait_ge(cH.sem, v)
                    e.dma_start(A[:, 24576:32768].rearrange("p (k t) -> p k t", k=4),
                                o2scr[:, 12:16, :]).then_inc(cH.sem, 16)
                s_ops.append(o2_load2)
                cH.n += 16
                o2t_box[0] = cH.n
        po_end = cPE.n

        g_ops = cur.setdefault("gpsimd", [])
        t_ops = cur.setdefault("tensor", [])
        v_ops = cur.setdefault("vector", [])
        sc_ops = cur.setdefault("scalar", [])
        s_ops = cur.setdefault("sync", [])
        g_ops.append(lambda e, v=po_end: e.wait_ge(cPE.sem, v))
        s_ops.append(lambda e, v=po_end: e.wait_ge(cPE.sem, v))
        t_ops.append(lambda e, v=o2t_box[0]: e.wait_ge(cH.sem, v))

        order = [(w8, ocs, tt) for w8 in range(NW8) for ocs in range(2)
                 for tt in range(NCOL)]
        xr_tgt = {}
        xr_free = {}
        def xr_dma(i):
            w8, ocs, tt = order[i]
            oc = w8 * 2 + ocs
            j = i % 4
            if j in xr_free:
                c, v = xr_free[j]
                s_ops.append(lambda e, c=c, v=v: e.wait_ge(c.sem, v))
            def f(e, oc=oc, tt=tt, j=j, s=s):
                e.dma_start(C_xr(j), xb_in[s][:, tt, oc, :]).then_inc(cH.sem, 16)
            s_ops.append(f)
            cH.n += 16
            xr_tgt[i] = cH.n
        for i in range(3):
            xr_dma(i)

        dma_done = {}
        consumed = {}
        def wdma_b5(j):
            if j >= 2:
                need = consumed[j - 2]
                g_ops.append(lambda e, v=need: e.wait_ge(cPE.sem, v))
            dst = C_wbuf(j % 2)[:].rearrange("p (k o) -> p k o", k=16)
            def f(e, dst=dst, j=j):
                e.dma_start(dst, w_in["wp"][j]).then_inc(cG.sem, 16)
            g_ops.append(f)
            cG.n += 16
            dma_done[j] = cG.n
        for j in range(2):
            wdma_b5(j)

        ev_free = {}
        roti = [0]
        i_lin = 0
        for j in range(NW8):
            if j >= 2:
                wdma_b5(j)
            t_ops.append(lambda e, v=dma_done[j]: e.wait_ge(cG.sem, v))
            wv = C_wbuf(j % 2)[:].rearrange("p (k o) -> p k o", k=16)
            for ocs in range(2):
                for tt in range(NCOL):
                    pm_i = roti[0]
                    roti[0] = (roti[0] + 1) % 3
                    pm_wait(t_ops, pm_i)
                    pm = PM[pm_i]
                    def grp(e, pm=pm, wv=wv, ocs=ocs, tt=tt):
                        ins = None
                        for kd in range(NKD):
                            ins = e.matmul(pm[:], wv[:, kd, ocs * 128:(ocs + 1) * 128],
                                           chunk_col(A, kd, tt),
                                           start=(kd == 0), stop=(kd == NKD - 1))
                        ins.then_inc(cPE.sem, 1)
                    t_ops.append(grp)
                    cPE.n += 1
                    pev = cPE.n
                    w8_, ocs_, tt_ = order[i_lin]
                    oc = w8_ * 2 + ocs_
                    jj = i_lin % 4
                    if i_lin + 3 < len(order):
                        xr_dma(i_lin + 3)
                    if jj in ev_free:
                        for c, v in ev_free[jj]:
                            v_ops.append(lambda e, c=c, v=v: e.wait_ge(c.sem, v))
                    def ad(e, pm=pm, jj=jj, v=pev, hx=xr_tgt[i_lin]):
                        e.wait_ge(cPE.sem, v)
                        e.wait_ge(cH.sem, hx)
                        e.tensor_tensor(C_ev(jj), pm[:], C_xr(jj), ALU.add).then_inc(cDV.sem, 1)
                    v_ops.append(ad)
                    cDV.n += 1
                    dvv = cDV.n
                    pm_state[pm_i] = (cDV, dvv)
                    xr_free[jj] = (cDV, dvv)
                    def st(e, oc=oc, tt=tt_, jj=jj, v=dvv):
                        e.wait_ge(cDV.sem, v)
                        e.dma_start(x2scr[:, oc, tt * CW:(tt + 1) * CW],
                                    C_ev(jj)).then_inc(cH.sem, 16)
                    s_ops.append(st)
                    cH.n += 16
                    h_v = cH.n
                    def cv(e, oc=oc, tt=tt_, jj=jj, v=dvv):
                        e.wait_ge(cDV.sem, v)
                        e.copy(chunk_col(B, oc, tt), C_ev(jj)).then_inc(cAC.sem, 1)
                    sc_ops.append(cv)
                    cAC.n += 1
                    ev_free[jj] = [(cH, h_v), (cAC, cAC.n)]
                    i_lin += 1
            consumed[j] = cPE.n
        end_waits(cur)
        run_block(cur)
        check_stop('B5')

        if tap_x2 is not None and s == 0:
            cur = {}
            tap_dma(cur, tap_x2, x2scr.rearrange("p k t -> p (k t)"))
            end_waits(cur)
            run_block(cur)

        cur = {}
        ws_up = {"j": 0, "consumed": {}, "dma_done": {}}
        prefetch_w(cur, ws_up, w_in["wup"], [0, 1], C_wbuf, 2)
        g_ops = cur.setdefault("gpsimd", [])
        def ffn_pf0(e):
            e.dma_start(C_wbuf(2)[:].rearrange("p (k o) -> p k o", k=16),
                        w_in["wdn"][0]).then_inc(cG.sem, 16)
            e.dma_start(C_wbuf(3)[:].rearrange("p (k o) -> p k o", k=16),
                        w_in["wex"][0]).then_inc(cG.sem, 16)
        g_ops.append(ffn_pf0)
        cG.n += 32
        ffn_pf = cG.n
        t_ops = cur.setdefault("tensor", [])
        v_ops = cur.setdefault("vector", [])
        sc_ops = cur.setdefault("scalar", [])
        s_ops = cur.setdefault("sync", [])
        ln2col = {}
        emit_ln_stats(cur, lambda kd, tt: chunk_col(B, kd, tt), C_eb5, ln2_finish)
        wgv = wgateS[:].rearrange("p (k o) -> p k o", o=8)
        for tt in range(NCOL):
            for c, v in ln2col[tt]:
                t_ops.append(lambda e, c=c, v=v: e.wait_ge(c.sem, v))
            pm_wait(t_ops, 3)
            def lmm(e, tt=tt):
                ins = None
                for kd in range(NKD):
                    ins = e.matmul(PM[3][0:8, :], wgv[:, kd, :],
                                   chunk_col(B, kd, tt),
                                   start=(kd == 0), stop=(kd == NKD - 1))
                ins.then_inc(cPE.sem, 1)
            t_ops.append(lmm)
            cPE.n += 1
            pev = cPE.n
            def lcp(e, tt=tt, v=pev):
                e.wait_ge(cPE.sem, v)
                e.tensor_copy(Lsb, PM[3][0:8, :]).then_inc(cDV.sem, 1)
            v_ops.append(lcp)
            cDV.n += 1
            pm_state[3] = (cDV, cDV.n)
            acv = cDV.n
            t_ops.append(lambda e, v=acv: e.wait_ge(cDV.sem, v))
            for jj in range(4):
                pt_i = jj % 2
                def ltr(e, jj=jj, pt_i=pt_i):
                    e.matmul(PT[pt_i][0:1, 0:2].bitcast(F32), zeroS[:, 0:1],
                             zeroS[0:128, 0:1], start=True, stop=True)
                    e.transpose(PT[pt_i][:, 0:8], Lsb[:, jj * 128:(jj + 1) * 128],
                                identS[0:8, 0:8])
                    e.matmul(PS0[0:1, 0:1], zeroS[:, 0:1], zeroS[0:128, 0:1],
                             start=True, stop=True).then_inc(cPE.sem, 1)
                t_ops.append(ltr)
                cPE.n += 1
                pv2 = cPE.n
                g = tt * 4 + jj
                def lto(e, g=g, pt_i=pt_i, v=pv2):
                    e.wait_ge(cPE.sem, v)
                    e.copy(Ltok[:, g * 8:(g + 1) * 8],
                           PT[pt_i][:, 0:8]).then_inc(cAC.sem, 1)
                sc_ops.append(lto)
                cAC.n += 1
                t_ops.append(lambda e, v=cAC.n: e.wait_ge(cAC.sem, v))
        lt_done = cAC.n
        emit_proj(cur, w_in["wup"], list(range(NW8)), C_wbuf,
                  lambda kd, tt: chunk_col(B, kd, tt), list(range(NCOL)),
                  lambda w8, ocs, tt: chunk_col(A, w8 * 2 + ocs, tt),
                  AF.Relu, rank1=False, rscale=False, rbc_fn=None, eb_fn=None,
                  wstate=ws_up, col_wait=ln2col)
        Lv = Ltok[:].rearrange("p (g e) -> p g e", e=8)
        n0 = cDV.n
        def top2(e, v=lt_done, n0=n0):
            e.wait_ge(cAC.sem, v)
            e.reduce_max(m1_s, Lv, AX.X).then_inc(cDV.sem, 1)
            e.wait_ge(cDV.sem, n0 + 1)
            e.tensor_tensor(T1_s[:].rearrange("p (g e) -> p g e", e=8), Lv,
                            m1_s[:].rearrange("p g -> p g ()").broadcast_to([128, 16, 8]),
                            ALU.subtract).then_inc(cDV.sem, 1)
            e.wait_ge(cDV.sem, n0 + 2)
            e.tensor_single_scalar(T2_s, T1_s, 0.0, ALU.is_ge).then_inc(cDV.sem, 1)
            e.wait_ge(cDV.sem, n0 + 3)
            e.scalar_tensor_tensor(T2_s, T2_s, -1e30, T1_s, ALU.mult,
                                   ALU.add).then_inc(cDV.sem, 1)
            e.wait_ge(cDV.sem, n0 + 4)
            e.reduce_max(m2_s, T2_s[:].rearrange("p (g e) -> p g e", e=8),
                         AX.X).then_inc(cDV.sem, 1)
        v_ops.append(top2)
        cDV.n += 5
        def t2e(e, v=cDV.n):
            e.wait_ge(cDV.sem, v)
            e.activation(m2_s, m2_s, AF.Exp).then_inc(cAC.sem, 1)
        sc_ops.append(t2e)
        cAC.n += 1
        def t2f(e, v=cAC.n, w1=cDV.n + 1, w2=cDV.n + 2):
            e.wait_ge(cAC.sem, v)
            e.tensor_scalar_add(s_s, m2_s, 1.0).then_inc(cDV.sem, 1)
            e.wait_ge(cDV.sem, w1)
            e.reciprocal(rinv_s, s_s).then_inc(cDV.sem, 1)
            e.wait_ge(cDV.sem, w2)
            e.tensor_tensor(RWS_s, s_s, rinv_s, ALU.mult).then_inc(cDV.sem, 1)
        v_ops.append(t2f)
        cDV.n += 3
        pm_wait(t_ops, 3)
        def rwt(e, v=cDV.n):
            e.wait_ge(cDV.sem, v)
            e.matmul(PM[3][0:1, 0:1], zeroS[:, 0:1], zeroS[0:128, 0:1],
                     start=True, stop=True)
            e.transpose(PM[3][0:16, 0:128], RWS_s, identF[:])
            e.matmul(PS0[0:1, 0:1], zeroS[:, 0:1], zeroS[0:128, 0:1],
                     start=True, stop=True).then_inc(cPE.sem, 1)
        t_ops.append(rwt)
        cPE.n += 1
        def rwc(e, v=cPE.n):
            e.wait_ge(cPE.sem, v)
            e.tensor_copy(row16, PM[3][0:16, 0:128]).then_inc(cDV.sem, 1)
        v_ops.append(rwc)
        cDV.n += 1
        pm_state[3] = (cDV, cDV.n)
        t_ops.append(lambda e, v=cDV.n: e.wait_ge(cDV.sem, v))
        def rwb(e, v=cDV.n):
            e.wait_ge(cDV.sem, v)
            e.dma_start(rwscr.rearrange("(g p) -> g p", g=16), row16).then_inc(cH.sem, 16)
        s_ops.append(rwb)
        cH.n += 16
        def rwb2(e, v=cH.n):
            e.wait_ge(cH.sem, v)
            e.dma_start(rw_row, rwscr.rearrange("(o t) -> o t", o=1)).then_inc(cH.sem, 16)
        s_ops.append(rwb2)
        cH.n += 16
        rw_tgt = cH.n
        if s == 0:
            tap_dma(cur, tap_rws, rwscr, [(cH, rw_tgt)])
            tap_dma(cur, tap_small, SMALL[:], [(cH, rw_tgt)])
            tap_dma(cur, tap_up, A[:, 0:NKD * T], [(cAC, cAC.n)])
        t_ops.append(lambda e, v=rw_tgt: e.wait_ge(cH.sem, v))
        build_rbc(cur, E_rwbc, rw_row)
        up_ac_end = cAC.n
        up_dv_end = cDV.n

        g_ops = cur.setdefault("gpsimd", [])
        t_ops = cur.setdefault("tensor", [])
        v_ops = cur.setdefault("vector", [])
        s_ops = cur.setdefault("sync", [])
        t_ops.append(lambda e, v=up_ac_end: e.wait_ge(cAC.sem, v))
        s_ops.append(lambda e, v=up_ac_end: e.wait_ge(cAC.sem, v))
        s_ops.append(lambda e, v=up_dv_end: e.wait_ge(cDV.sem, v))
        dma_done = {0: ffn_pf}
        consumed = {}
        up_last = ws_up["consumed"][NW8 - 1]
        def wdma_de(j):
            if j >= 2:
                need = consumed[j - 2]
                g_ops.append(lambda e, v=need: e.wait_ge(cPE.sem, v))
            elif j == 1:
                g_ops.append(lambda e, v=up_last: e.wait_ge(cPE.sem, v))
            base = 2 - (j % 2) * 2
            dd = C_wbuf(base)[:].rearrange("p (k o) -> p k o", k=16)
            de = C_wbuf(base + 1)[:].rearrange("p (k o) -> p k o", k=16)
            def f(e, dd=dd, de=de, j=j):
                e.dma_start(dd, w_in["wdn"][j]).then_inc(cG.sem, 16)
                e.dma_start(de, w_in["wex"][j]).then_inc(cG.sem, 16)
            g_ops.append(f)
            cG.n += 32
            dma_done[j] = cG.n
        wdma_de(1)

        order = [(w8, ocs, tt) for w8 in range(NW8) for ocs in range(2)
                 for tt in range(NCOL)]
        xr_tgt = {}
        xr_free = {}
        def x2_dma(i):
            w8, ocs, tt = order[i]
            oc = w8 * 2 + ocs
            j = i % 4
            if j in xr_free:
                c, v = xr_free[j]
                s_ops.append(lambda e, c=c, v=v: e.wait_ge(c.sem, v))
            def f(e, oc=oc, tt=tt, j=j):
                e.dma_start(E_xr(j), x2scr[:, oc, tt * CW:(tt + 1) * CW]).then_inc(cH.sem, 16)
            s_ops.append(f)
            cH.n += 16
            xr_tgt[i] = cH.n
        for i in range(3):
            x2_dma(i)

        ev_free = {}
        i_lin = 0
        for j in range(NW8):
            if j >= 2:
                wdma_de(j)
            t_ops.append(lambda e, v=dma_done[j]: e.wait_ge(cG.sem, v))
            base_j = 2 - (j % 2) * 2
            wdv = C_wbuf(base_j)[:].rearrange("p (k o) -> p k o", k=16)
            wev = C_wbuf(base_j + 1)[:].rearrange("p (k o) -> p k o", k=16)
            for ocs in range(2):
                for tt in range(NCOL):
                    pa_i = i_lin % 2
                    pb_i = 2 + (i_lin % 2)
                    pm_wait(t_ops, pa_i)
                    pm_wait(t_ops, pb_i)
                    pmA, pmB = PM[pa_i], PM[pb_i]
                    def grp(e, pmA=pmA, pmB=pmB, wdv=wdv, wev=wev, ocs=ocs, tt=tt,
                            jw=j):
                        for kd in range(NKD):
                            e.matmul(pmA[:], wdv[:, kd, ocs * 128:(ocs + 1) * 128],
                                     chunk_col(A, kd, tt),
                                     start=(kd == 0), stop=(kd == NKD - 1))
                        ins = None
                        for kd in range(NKD):
                            ins = e.matmul(pmB[:],
                                           wev[:, kd, ocs * 128:(ocs + 1) * 128],
                                           chunk_col(B, kd, tt),
                                           start=(kd == 0),
                                           stop=(kd == NKD - 1))
                        ins.then_inc(cPE.sem, 1)
                    t_ops.append(grp)
                    cPE.n += 1
                    pev = cPE.n
                    oc = order[i_lin][0] * 2 + ocs
                    jj = i_lin % 4
                    if i_lin + 3 < len(order):
                        x2_dma(i_lin + 3)
                    if jj in ev_free:
                        c, v = ev_free[jj]
                        v_ops.append(lambda e, c=c, v=v: e.wait_ge(c.sem, v))
                    def comb(e, pmA=pmA, pmB=pmB, tt=tt, jj=jj, v=pev,
                             hx=xr_tgt[i_lin]):
                        e.wait_ge(cPE.sem, v)
                        e.wait_ge(cH.sem, hx)
                        e.tensor_tensor(C_ev2(jj), pmB[:], E_rwbc(tt), ALU.mult)
                        e.tensor_tensor(C_ev2(jj), pmA[:], C_ev2(jj), ALU.add)
                        e.tensor_tensor(C_ev2(jj), E_xr(jj), C_ev2(jj),
                                        ALU.add).then_inc(cDV.sem, 1)
                    v_ops.append(comb)
                    cDV.n += 1
                    dvv = cDV.n
                    pm_state[pa_i] = (cDV, dvv)
                    pm_state[pb_i] = (cDV, dvv)
                    xr_free[jj] = (cDV, dvv)
                    def ostt(e, oc=oc, tt=tt, jj=jj, v=dvv, s=s):
                        e.wait_ge(cDV.sem, v)
                        e.dma_start(out_t[s][oc * 128:(oc + 1) * 128,
                                             tt * CW:(tt + 1) * CW],
                                    C_ev2(jj)).then_inc(cH.sem, 16)
                    s_ops.append(ostt)
                    cH.n += 16
                    ev_free[jj] = (cH, cH.n)
                    i_lin += 1
            consumed[j] = cPE.n
        end_waits(cur)
        run_block(cur)
        check_stop('SEQ' + str(s))
    except _StopEmit:
        pass

    ctx.close()
    return nc


def _block_weight(w):
    return np.ascontiguousarray(w.reshape(16, 128, 8, 256).transpose(2, 1, 0, 3))


def _make_consts():
    masks = np.zeros((128, 2048), dtype=ml_dtypes.bfloat16)
    jk = np.arange(128)[:, None]
    iq = np.arange(512)[None, :]
    for j in range(4):
        masks[:, j * 512:(j + 1) * 512] = (jk <= iq - 128 * j).astype(ml_dtypes.bfloat16)
    ident = np.eye(128, dtype=ml_dtypes.bfloat16)
    identf = np.eye(128, dtype=np.float32)
    onesc = np.ones((128, 1), dtype=ml_dtypes.bfloat16)
    onesr = np.ones((1, 128), dtype=np.float32)
    return masks, ident, identf, onesc, onesr


def kernel(L0_attn_x, L0_attn_Wq, L0_attn_Wk, L0_attn_Wv, L0_attn_Wg, L0_attn_Wp,
           L0_ffn_W_up, L0_ffn_W_down, L0_ffn_W_gate, L0_ffn_W_expert):
    x = np.asarray(L0_attn_x, dtype=np.float32)
    wq = _block_weight(np.ascontiguousarray(np.asarray(L0_attn_Wq, np.float32).T))
    wk = _block_weight(np.ascontiguousarray(np.asarray(L0_attn_Wk, np.float32).T))
    wv = _block_weight(np.ascontiguousarray(np.asarray(L0_attn_Wv, np.float32).T))
    wg = _block_weight(np.ascontiguousarray(np.asarray(L0_attn_Wg, np.float32).T))
    wp = _block_weight(np.ascontiguousarray(np.asarray(L0_attn_Wp, np.float32).T))
    wup = _block_weight(np.ascontiguousarray(np.asarray(L0_ffn_W_up, np.float32).T))
    wdn = _block_weight(np.ascontiguousarray(np.asarray(L0_ffn_W_down, np.float32).T))
    wex = _block_weight(np.ascontiguousarray(np.asarray(L0_ffn_W_expert, np.float32)))
    wgt = np.ascontiguousarray(np.asarray(L0_ffn_W_gate, np.float32).T
                               .reshape(16, 128, 8).transpose(1, 0, 2).reshape(128, 128))
    masks, ident, identf, onesc, onesr = _make_consts()

    nc = build_nc()
    in_maps = []
    for c in range(8):
        xs = x[2 * c:2 * c + 2]
        xbl = np.ascontiguousarray(
            xs.transpose(0, 2, 1).reshape(NSEQ, 16, 128, NCOL, CW)
            .transpose(0, 2, 3, 1, 4))
        in_maps.append(dict(xb=xbl, wq=wq, wk=wk, wv=wv, wg=wg, wp=wp,
                            wup=wup, wdn=wdn, wex=wex, wgate=wgt,
                            masks=masks, ident=ident, identf=identf,
                            onesc=onesc, onesr=onesr))
    res = run_bass_kernel_spmd(nc, in_maps, core_ids=list(range(8)))
    out = np.empty((16, T, D), dtype=np.float32)
    for c in range(8):
        ot = np.asarray(res.results[c]["outT"])
        out[2 * c] = ot[0].T
        out[2 * c + 1] = ot[1].T
    return out


# version 18 (final)
# speedup vs baseline: 1.0236x; 1.0236x over previous
import numpy as np
import ml_dtypes
import concourse.bass as bass
import concourse.mybir as mybir
from concourse.bass_utils import run_bass_kernel_spmd
from contextlib import ExitStack

F32 = mybir.dt.float32
BF16 = mybir.dt.bfloat16
AF = mybir.ActivationFunctionType
ALU = mybir.AluOpType
AX = mybir.AxisListType

D = 2048
T = 2048
NKD = 16
NCOL = 4
CW = 512
NW8 = 8
NSEQ = 2

LN_EPS = 1e-5
RMS_EPS = float(np.finfo(np.float32).eps)

DEBUG_TAPS = ()
STOP_AFTER = None


class _StopEmit(Exception):
    pass


class Ctr:
    def __init__(self, sem):
        self.sem = sem
        self.n = 0


def build_nc():
    nc = bass.Bass("TRN2", target_bir_lowering=False)

    xb_in = nc.dram_tensor("xb", [NSEQ, 128, NCOL, NKD, CW], F32, kind="ExternalInput").ap()
    wnames = ["wq", "wk", "wv", "wg", "wp", "wup", "wdn", "wex"]
    w_in = {n: nc.dram_tensor(n, [NW8, 128, NKD, 256], F32, kind="ExternalInput").ap()
            for n in wnames}
    wgate_in = nc.dram_tensor("wgate", [128, 128], F32, kind="ExternalInput").ap()
    masks_in = nc.dram_tensor("masks", [128, 2048], BF16, kind="ExternalInput").ap()
    ident_in = nc.dram_tensor("ident", [128, 128], BF16, kind="ExternalInput").ap()
    identf_in = nc.dram_tensor("identf", [128, 128], F32, kind="ExternalInput").ap()
    onesc_in = nc.dram_tensor("onesc", [128, 1], BF16, kind="ExternalInput").ap()
    onesr_in = nc.dram_tensor("onesr", [1, 128], F32, kind="ExternalInput").ap()
    out_t = nc.dram_tensor("outT", [NSEQ, D, T], F32, kind="ExternalOutput").ap()
    o2scr = nc.dram_tensor("o2scr", [128, NKD, T], BF16).ap()
    wkb = nc.dram_tensor("wkb", [NW8, 128, NKD, 256], BF16).ap()
    xnscr = nc.dram_tensor("xnscr", [128, NCOL, NKD, CW], BF16).ap()
    x2scr = nc.dram_tensor("x2scr", [128, NKD, T], F32).ap()
    rwscr = nc.dram_tensor("rwscr", [T], F32).ap()

    taps = {}
    def tapt(name, shape, dtype):
        if name in DEBUG_TAPS:
            taps[name] = nc.dram_tensor("tap_" + name, shape, dtype,
                                        kind="ExternalOutput").ap()
            return taps[name]
        return None

    tap_xhat = tapt("xhat", [128, NKD * T], BF16)
    tap_rows = tapt("rows", [2, T], F32)
    tap_q = tapt("q", [128, NKD * T], BF16)
    tap_k = tapt("k", [128, NKD * CW], BF16)
    tap_s = tapt("s", [128, 40 * CW], BF16)
    tap_vt = tapt("vt", [128, NKD * CW], BF16)
    tap_g = tapt("g", [128, 4 * T], BF16)
    tap_o2 = tapt("o2", [128, NKD * T], BF16)
    tap_x2 = tapt("x2", [128, NKD * T], F32)
    tap_xn2 = tapt("xn2", [128, NKD * T], BF16)
    tap_rws = tapt("rws", [T], F32)
    tap_up = tapt("up", [128, NKD * T], BF16)
    tap_small = tapt("small", [128, 512], F32)

    ctx = ExitStack()
    sb = lambda *a: ctx.enter_context(nc.sbuf_tensor(*a))
    psa = lambda *a: ctx.enter_context(nc.psum_tensor(*a))

    A = sb("A", [128, 32768], BF16)
    B = sb("B", [128, 32768], BF16)
    C = sb("C", [128, 20480], BF16)
    E = sb("E", [128, 8192], BF16)
    masksS = sb("masksS", [128, 2048], BF16)
    identS = sb("identS", [128, 128], BF16)
    identF = sb("identF", [128, 128], F32)
    onescS = sb("onescS", [128, 1], BF16)
    zeroS = sb("zeroS", [128, 1], BF16)
    wgateS = sb("wgateS", [128, 128], BF16)
    ROWS = sb("ROWS", [128, 4096], F32)
    SMALL = sb("SMALL", [128, 512], F32)

    PM = [psa(f"PM{i}", [128, 512], F32) for i in range(4)]
    PS0 = psa("PS0", [1, 512], F32)
    PS1 = psa("PS1", [1, 512], F32)
    PT = [psa(f"PT{i}", [128, 128], BF16) for i in range(2)]

    cG = Ctr(ctx.enter_context(nc.semaphore("sG")))
    cH = Ctr(ctx.enter_context(nc.semaphore("sH")))
    cPE = Ctr(ctx.enter_context(nc.semaphore("sPE")))
    cDV = Ctr(ctx.enter_context(nc.semaphore("sDV")))
    cAC = Ctr(ctx.enter_context(nc.semaphore("sAC")))
    cPL = Ctr(ctx.enter_context(nc.semaphore("sPL")))

    def chunk_col(arena, kd, tt):
        return arena[:, kd * T + tt * CW: kd * T + tt * CW + CW]

    def xhat_col(kd, tt):
        return A[:, tt * 8192 + kd * CW: tt * 8192 + kd * CW + CW]

    def A_kq_tile(oc):
        return A[:, oc * CW:(oc + 1) * CW]

    def A_xq(j):
        return A[:, 16384 + j * 8192: 16384 + (j + 1) * 8192]

    def A_wbuf(j):
        return A[:, 8192 + j * 4096: 8192 + (j + 1) * 4096]

    def B_vt(tc):
        return B[:, tc * CW:(tc + 1) * CW]

    def B_g(oc4):
        return B[:, 8192 + oc4 * T: 8192 + (oc4 + 1) * T]

    def B_wbuf(j):
        return B[:, 16384 + j * 4096: 16384 + (j + 1) * 4096]

    def B_rbc(tt):
        return B[:, 24576 + tt * 1024: 24576 + (tt + 1) * 1024].bitcast(F32)

    def B_eb(j):
        return B[:, 28672 + j * 512: 28672 + (j + 1) * 512]

    def B_eb2(j):
        return B[:, 30720 + j * 512: 30720 + (j + 1) * 512]

    def E_rbc(tt):
        return E[:, tt * 512:(tt + 1) * 512]

    def E_mbc(tt):
        return E[:, 2048 + tt * 512: 2048 + (tt + 1) * 512]

    def E_bc1(tt):
        j = tt % 2
        return E[:, j * 1024:(j + 1) * 1024].bitcast(F32)

    def E_bc2(tt):
        j = tt % 2
        return E[:, 2048 + j * 1024: 2048 + (j + 1) * 1024].bitcast(F32)

    def E_ev(j):
        return E[:, 4096 + j * 1024: 4096 + (j + 1) * 1024].bitcast(F32)

    def E_xr(j):
        return E[:, 4096 + j * 1024: 4096 + (j + 1) * 1024].bitcast(F32)

    def E_rwbc(tt):
        return E[:, tt * 1024:(tt + 1) * 1024].bitcast(F32)

    def C_wbuf(j):
        return C[:, j * 4096:(j + 1) * 4096]

    def C_rbc(tt):
        return C[:, 8192 + tt * 1024: 8192 + (tt + 1) * 1024].bitcast(F32)

    def C_eb(j):
        return C[:, 12288 + j * 512: 12288 + (j + 1) * 512]

    C_xr = C_rbc

    def C_ev(j):
        return C[:, 12288 + j * 1024: 12288 + (j + 1) * 1024].bitcast(F32)

    def C_eb5(j):
        return C[:, 16384 + j * 512: 16384 + (j + 1) * 512]

    def C_ev2(j):
        return C[:, 16384 + j * 1024: 16384 + (j + 1) * 1024].bitcast(F32)

    def C_stile(idx):
        return C[:, idx * CW:(idx + 1) * CW]

    def mask_ap(j):
        return masksS[:, j * CW:(j + 1) * CW]

    onesr = ROWS[0:1, 0:128]
    r0 = [ROWS[0:1, 128 + i * 512:128 + (i + 1) * 512] for i in range(3)]
    R_row = ROWS[0:1, 1664:3712]
    rw_row = R_row
    m_negb = ROWS[32:33, 0:1024].bitcast(BF16)
    m_negb64 = ROWS[64:65, 0:1024].bitcast(BF16)
    rs_store = {
        "wk": ROWS[32:33, 1024:2048].bitcast(BF16),
        "wup": ROWS[32:33, 2048:3072].bitcast(BF16),
        "wex": ROWS[32:33, 3072:4096].bitcast(BF16),
        "wq": ROWS[64:65, 1024:2048].bitcast(BF16),
        "wv": ROWS[64:65, 2048:3072].bitcast(BF16),
        "wg": ROWS[64:65, 3072:4096].bitcast(BF16),
    }
    rs_part = {"wk": 32, "wup": 32, "wex": 32, "wq": 64, "wv": 64, "wg": 64}

    rs_gate_b = SMALL[32:33, 448:452].bitcast(BF16)
    Ltok = SMALL[:, 0:64].bitcast(BF16)
    m1_s = SMALL[:, 64:80]
    m2_s = SMALL[:, 80:96]
    s_s = SMALL[:, 96:112]
    rinv_s = SMALL[:, 112:128]
    T1_s = SMALL[:, 128:256]
    T2_s = SMALL[:, 256:384]
    RWS_s = SMALL[:, 496:512]
    r2pg = SMALL[:, 480:496]
    Rpg = SMALL[:, 384:400]
    row16 = SMALL[0:16, 128:256]
    Lsb = SMALL[0:8, 128:384].bitcast(BF16)

    s_off = {}
    _n = 0
    for kc in range(16):
        for qq in range(kc // 4, 4):
            s_off[(kc, qq)] = _n
            _n += 1

    stop_box = [False]

    def check_stop(name):
        if STOP_AFTER is not None and STOP_AFTER == name:
            raise _StopEmit()

    def run_block(streams):
        with nc.Block(no_gpsimd_drain=True) as blk:
            for name in ["gpsimd", "tensor", "vector", "scalar", "sync"]:
                ops = streams.get(name)
                if ops:
                    def runner(eng, ops=ops):
                        for f in ops:
                            f(eng)
                    getattr(blk, name)(runner)

    def end_waits(streams):
        gw, hw = cG.n, cH.n
        streams.setdefault("gpsimd", []).append(lambda e, v=gw: e.wait_ge(cG.sem, v))
        streams.setdefault("sync", []).append(lambda e, v=hw: e.wait_ge(cH.sem, v))

    def tap_dma(streams, tap, src_ap, waits=()):
        if tap is None:
            return
        s_ops = streams.setdefault("sync", [])
        for c, v in waits:
            s_ops.append(lambda e, c=c, v=v: e.wait_ge(c.sem, v))
        def f(e):
            e.dma_start(tap, src_ap).then_inc(cH.sem, 16)
        s_ops.append(f)
        cH.n += 16

    pm_state = {}

    def pm_wait(t_ops, pid):
        if pid in pm_state:
            c, v = pm_state[pid]
            t_ops.append(lambda e, c=c, v=v: e.wait_ge(c.sem, v))

    rs_guard = [0]
    ln_ps_mark = [0]

    def prefetch_w(streams, wstate, w_hbm, w8_list, wbuf_fn, n, wait=None):
        g_ops = streams.setdefault("gpsimd", [])
        if wait is not None:
            c, v = wait
            g_ops.append(lambda e, c=c, v=v: e.wait_ge(c.sem, v))
        j0 = wstate["j"]
        for jj in range(n):
            j = j0 + jj
            if j >= 2 and (j - 2) in wstate["consumed"]:
                need = wstate["consumed"][j - 2]
                g_ops.append(lambda e, v=need: e.wait_ge(cPE.sem, v))
            dst = wbuf_fn(j % 2)[:].rearrange("p (k o) -> p k o", k=16)
            src = w_hbm[w8_list[jj]]
            def f(e, dst=dst, src=src):
                e.dma_start(dst, src).then_inc(cG.sem, 16)
            g_ops.append(f)
            cG.n += 16
            wstate["dma_done"][j] = cG.n
        wstate["issued"] = j0 + n

    def emit_proj(streams, w_hbm, w8_list, wbuf_fn, rhs_fn, cols, dest_fn,
                  act_func, rank1, rscale, rbc_fn, eb_fn,
                  wstate=None, eb_state=None, wqueue="gpsimd", wctr=None,
                  col_wait=None, rs_name=None, rs_reuse=False):
        g_ops = streams.setdefault(wqueue, [])
        t_ops = streams.setdefault("tensor", [])
        v_ops = streams.setdefault("vector", [])
        sc_ops = streams.setdefault("scalar", [])
        if wctr is None:
            wctr = cG
        col_seen = set()
        dest_marks = {}
        if rank1:
            rs_row = rs_store[rs_name]
            mrow = m_negb if rs_part[rs_name] == 32 else m_negb64

        if wstate is None:
            wstate = {"j": 0, "consumed": {}, "dma_done": {}}
        wstate.setdefault("issued", wstate["j"])
        if eb_state is None:
            eb_state = {"free": {}, "rot": 0}
        j0 = wstate["j"]
        consumed = wstate["consumed"]
        dma_done = wstate["dma_done"]
        eb_free = eb_state["free"]
        rot = [0]

        def wdma(j):
            if j >= 2:
                need = consumed[j - 2]
                g_ops.append(lambda e, v=need: e.wait_ge(cPE.sem, v))
            dst = wbuf_fn(j % 2)[:].rearrange("p (k o) -> p k o", k=16)
            src = w_hbm[w8_list[j - j0]]
            def f(e, dst=dst, src=src):
                e.dma_start(dst, src).then_inc(wctr.sem, 16)
            g_ops.append(f)
            wctr.n += 16
            dma_done[j] = wctr.n

        for j in range(j0, j0 + len(w8_list)):
            if j >= wstate["issued"]:
                wdma(j)
                wstate["issued"] = j + 1
            wv = wbuf_fn(j % 2)[:].rearrange("p (k o) -> p k o", k=16)
            t_ops.append(lambda e, v=dma_done[j], c=wctr: e.wait_ge(c.sem, v))
            w8v = w8_list[j - j0]
            if rank1 and not rs_reuse:
                if rs_guard[0]:
                    t_ops.append(lambda e, v=rs_guard[0]: e.wait_ge(cDV.sem, v))
                def rsmm(e, wv=wv):
                    ins = None
                    for kd in range(NKD):
                        ins = e.matmul(PS0[0:1, 0:256], onescS[:, 0:1], wv[:, kd, :],
                                       start=(kd == 0), stop=(kd == NKD - 1))
                    ins.then_inc(cPE.sem, 1)
                t_ops.append(rsmm)
                cPE.n += 1
                pe_rs = cPE.n
                dstslice = rs_row[0:1, w8v * 256:(w8v + 1) * 256]
                def rscp(e, dstslice=dstslice, v=pe_rs):
                    e.wait_ge(cPE.sem, v)
                    e.tensor_copy(dstslice, PS0[0:1, 0:256]).then_inc(cDV.sem, 1)
                v_ops.append(rscp)
                cDV.n += 1
                rs_guard[0] = cDV.n
                t_ops.append(lambda e, v=cDV.n: e.wait_ge(cDV.sem, v))

            for oc_sub in range(2):
                for tt in cols:
                    if col_wait is not None and tt not in col_seen:
                        col_seen.add(tt)
                        cw = col_wait[tt]
                        for c, v in (cw if isinstance(cw, list) else [cw]):
                            t_ops.append(lambda e, c=c, v=v: e.wait_ge(c.sem, v))
                    pm_i = rot[0]
                    rot[0] = (rot[0] + 1) % 3
                    pm_wait(t_ops, pm_i)
                    pm = PM[pm_i]
                    lhs = wv[:, :, oc_sub * 128:(oc_sub + 1) * 128]
                    def grp(e, pm=pm, lhs=lhs, tt=tt, w8v=w8v, oc_sub=oc_sub,
                            do_r1=rank1):
                        ins = None
                        for kd in range(NKD):
                            ins = e.matmul(pm[:], lhs[:, kd, :], rhs_fn(kd, tt),
                                           start=(kd == 0),
                                           stop=(kd == NKD - 1 and not do_r1))
                        if do_r1:
                            ins = e.matmul(
                                pm[:], rs_row[0:1, w8v * 256 + oc_sub * 128:
                                              w8v * 256 + (oc_sub + 1) * 128],
                                mrow[0:1, tt * CW:(tt + 1) * CW],
                                start=False, stop=True)
                        ins.then_inc(cPE.sem, 1)
                    t_ops.append(grp)
                    cPE.n += 1
                    pe_grp = cPE.n
                    dest = dest_fn(w8_list[j - j0], oc_sub, tt)
                    if rscale:
                        ebs = eb_state["rot"]
                        eb_state["rot"] = (ebs + 1) % 4
                        eb = eb_fn(ebs)
                        if ebs in eb_free:
                            v_ops.append(lambda e, v=eb_free[ebs]: e.wait_ge(cAC.sem, v))
                        def ev(e, pm=pm, eb=eb, tt=tt, v=pe_grp):
                            e.wait_ge(cPE.sem, v)
                            e.tensor_tensor(eb, pm[:], rbc_fn(tt), ALU.mult).then_inc(cDV.sem, 1)
                        v_ops.append(ev)
                        cDV.n += 1
                        pm_state[pm_i] = (cDV, cDV.n)
                        dv_v = cDV.n
                        def ac(e, eb=eb, dest=dest, v=dv_v, fn=act_func):
                            e.wait_ge(cDV.sem, v)
                            e.activation(dest, eb, fn).then_inc(cAC.sem, 1)
                        sc_ops.append(ac)
                        cAC.n += 1
                        eb_free[ebs] = cAC.n
                        dest_marks[(w8v, oc_sub)] = cAC.n
                    else:
                        def ac2(e, pm=pm, dest=dest, v=pe_grp, fn=act_func):
                            e.wait_ge(cPE.sem, v)
                            if fn is None:
                                e.copy(dest, pm[:]).then_inc(cAC.sem, 1)
                            else:
                                e.activation(dest, pm[:], fn).then_inc(cAC.sem, 1)
                        sc_ops.append(ac2)
                        cAC.n += 1
                        pm_state[pm_i] = (cAC, cAC.n)
                        dest_marks[(w8v, oc_sub)] = cAC.n
            consumed[j] = cPE.n
        wstate["j"] = j0 + len(w8_list)
        return dest_marks

    def build_rbc(streams, rbc_fn, src_row, cols=range(NCOL)):
        t_ops = streams.setdefault("tensor", [])
        v_ops = streams.setdefault("vector", [])
        for tt in cols:
            pm_wait(t_ops, 3)
            def bmm(e, tt=tt):
                e.matmul(PM[3][:], onesr, src_row[0:1, tt * CW:(tt + 1) * CW],
                         start=True, stop=True).then_inc(cPE.sem, 1)
            t_ops.append(bmm)
            cPE.n += 1
            v = cPE.n
            def cp(e, tt=tt, v=v):
                e.wait_ge(cPE.sem, v)
                e.tensor_copy(rbc_fn(tt), PM[3][:]).then_inc(cDV.sem, 1)
            v_ops.append(cp)
            cDV.n += 1
            pm_state[3] = (cDV, cDV.n)
            t_ops.append(lambda e, v=cDV.n: e.wait_ge(cDV.sem, v))

    def emit_ln_stats(streams, src_fn, eb_fn, finish_fn, col_wait=None):
        sc_ops = streams.setdefault("scalar", [])
        t_ops = streams.setdefault("tensor", [])
        slot_reader = {}
        for tt in range(NCOL):
            if col_wait is not None:
                sc_ops.append(lambda e, v=col_wait[tt]: e.wait_ge(cG.sem, v))
                t_ops.append(lambda e, v=col_wait[tt]: e.wait_ge(cG.sem, v))
            if tt > 0:
                t_ops.append(lambda e, v=ln_ps_mark[0]: e.wait_ge(cDV.sem, v))
            for kd in range(NKD):
                ebs = kd % 4
                if ebs in slot_reader:
                    sc_ops.append(lambda e, v=slot_reader[ebs]: e.wait_ge(cPE.sem, v))
                def sq(e, kd=kd, tt=tt, ebs=ebs):
                    e.activation(eb_fn(ebs), src_fn(kd, tt), AF.Square).then_inc(cAC.sem, 1)
                sc_ops.append(sq)
                cAC.n += 1
                ac_v = cAC.n
                def mm2(e, kd=kd, tt=tt, ebs=ebs, v=ac_v):
                    e.matmul(PS0[0:1, :], onescS[:, 0:1], src_fn(kd, tt),
                             start=(kd == 0), stop=(kd == NKD - 1))
                    e.wait_ge(cAC.sem, v)
                    e.matmul(PS1[0:1, :], onescS[:, 0:1], eb_fn(ebs),
                             start=(kd == 0), stop=(kd == NKD - 1)).then_inc(cPE.sem, 1)
                t_ops.append(mm2)
                cPE.n += 1
                slot_reader[ebs] = cPE.n
            finish_fn(tt)
            rs_guard[0] = max(rs_guard[0], cDV.n)

    cur = {}

    def ln1_finish(tt):
        v_ops = cur.setdefault("vector", [])
        sc_ops = cur.setdefault("scalar", [])
        pe_v = cPE.n
        def rows1(e, tt=tt, v=pe_v):
            e.wait_ge(cPE.sem, v)
            e.tensor_scalar_mul(r0[0], PS0[0:1, :], 1.0 / D)
            e.tensor_tensor(r0[1], r0[0], r0[0], ALU.mult)
            e.tensor_scalar_mul(r0[2], PS1[0:1, :], 1.0 / D)
            e.tensor_tensor(r0[2], r0[2], r0[1], ALU.subtract)
            e.tensor_scalar_mul(r0[1], r0[2], 1.0 + RMS_EPS)
            e.tensor_scalar_add(r0[1], r0[1],
                                RMS_EPS * LN_EPS).then_inc(cDV.sem, 1)
        v_ops.append(rows1)
        cDV.n += 1
        ln_ps_mark[0] = cDV.n
        dv_v = cDV.n
        def rows1b(e, tt=tt, v=dv_v):
            e.wait_ge(cDV.sem, v)
            e.activation(r0[1], r0[1], AF.Ln)
            e.activation(R_row[0:1, tt * CW:(tt + 1) * CW], r0[1],
                         AF.Exp, scale=-0.5).then_inc(cAC.sem, 1)
        sc_ops.append(rows1b)
        cAC.n += 1
        ac_v = cAC.n
        t_ops = cur.setdefault("tensor", [])
        s_ops = cur.setdefault("sync", [])
        pm_wait(t_ops, 3)
        pm_wait(t_ops, 2)
        def bc1(e, tt=tt, v=cDV.n, va=ac_v):
            e.wait_ge(cDV.sem, v)
            e.matmul(PM[3][:], onesr, r0[0], start=True,
                     stop=True).then_inc(cPE.sem, 1)
            e.wait_ge(cAC.sem, va)
            e.matmul(PM[2][:], onesr, R_row[0:1, tt * CW:(tt + 1) * CW],
                     start=True, stop=True).then_inc(cPE.sem, 1)
        t_ops.append(bc1)
        cPE.n += 2
        def bc1c(e, tt=tt, v=cPE.n):
            e.wait_ge(cPE.sem, v)
            e.tensor_copy(E_mbc(tt), PM[3][:])
            e.tensor_copy(E_rbc(tt), PM[2][:]).then_inc(cDV.sem, 1)
        v_ops.append(bc1c)
        cDV.n += 1
        pm_state[3] = (cDV, cDV.n)
        pm_state[2] = (cDV, cDV.n)
        t_ops.append(lambda e, v=cDV.n: e.wait_ge(cDV.sem, v))
        g_ops2 = cur.setdefault("gpsimd", [])
        bc_ready = cDV.n
        for kd in range(11):
            last = (kd == 10)
            def xna(e, kd=kd, tt=tt, last=last):
                xt = xhat_col(kd, tt)
                e.tensor_tensor(xt, xt, E_mbc(tt), ALU.subtract)
                ins = e.tensor_tensor(xt, xt, E_rbc(tt), ALU.mult)
                if last:
                    ins.then_inc(cDV.sem, 1)
            v_ops.append(xna)
        cDV.n += 1
        g_ops2.append(lambda e, v=bc_ready: e.wait_ge(cDV.sem, v))
        for kd in range(11, NKD):
            last = (kd == NKD - 1)
            def xnap(e, kd=kd, tt=tt, last=last):
                xt = xhat_col(kd, tt)
                e.tensor_tensor(xt, xt, E_mbc(tt), ALU.subtract)
                ins = e.tensor_tensor(xt, xt, E_rbc(tt), ALU.mult)
                if last:
                    ins.then_inc(cPL.sem, 1)
            g_ops2.append(xnap)
        cPL.n += 1
        ln1col[tt] = [(cDV, cDV.n), (cPL, cPL.n)]
        def xst(e, tt=tt, v=cDV.n, vp=cPL.n):
            e.wait_ge(cDV.sem, v)
            e.wait_ge(cPL.sem, vp)
            e.dma_start(xnscr[:, tt],
                        A[:, tt * 8192:(tt + 1) * 8192]
                        .rearrange("p (k t) -> p k t", k=NKD)).then_inc(cH.sem, 16)
        s_ops.append(xst)
        cH.n += 16
        xnst_tgt[tt] = cH.n

    def ln2_finish(tt):
        v_ops = cur.setdefault("vector", [])
        sc_ops = cur.setdefault("scalar", [])
        t_ops = cur.setdefault("tensor", [])
        pe_v = cPE.n
        def rows2(e, tt=tt, v=pe_v):
            e.wait_ge(cPE.sem, v)
            e.tensor_scalar_mul(r0[0], PS0[0:1, :], 1.0 / D)
            e.tensor_tensor(r0[1], r0[0], r0[0], ALU.mult)
            e.tensor_scalar_mul(r0[2], PS1[0:1, :], 1.0 / D)
            e.tensor_tensor(r0[2], r0[2], r0[1], ALU.subtract)
            e.tensor_scalar_add(r0[1], r0[2], LN_EPS).then_inc(cDV.sem, 1)
        v_ops.append(rows2)
        cDV.n += 1
        ln_ps_mark[0] = cDV.n
        dv_v = cDV.n
        def rows2b(e, v=dv_v):
            e.wait_ge(cDV.sem, v)
            e.activation(r0[1], r0[1], AF.Ln)
            e.activation(r0[1], r0[1], AF.Exp, scale=-0.5).then_inc(cAC.sem, 1)
        sc_ops.append(rows2b)
        cAC.n += 1
        ac_v = cAC.n
        pm_wait(t_ops, 3)
        pm_wait(t_ops, 2)
        def bco(e, va=ac_v):
            e.wait_ge(cAC.sem, va)
            e.matmul(PM[3][:], onesr, r0[0], start=True,
                     stop=True).then_inc(cPE.sem, 1)
            e.matmul(PM[2][:], onesr, r0[1], start=True,
                     stop=True).then_inc(cPE.sem, 1)
        t_ops.append(bco)
        cPE.n += 2
        def cpo(e, tt=tt, v=cPE.n):
            e.wait_ge(cPE.sem, v)
            e.tensor_copy(E_mbc(tt), PM[3][:])
            e.tensor_copy(E_rbc(tt), PM[2][:]).then_inc(cDV.sem, 1)
        v_ops.append(cpo)
        cDV.n += 1
        pm_state[3] = (cDV, cDV.n)
        pm_state[2] = (cDV, cDV.n)
        t_ops.append(lambda e, v=cDV.n: e.wait_ge(cDV.sem, v))
        g_ops2 = cur.setdefault("gpsimd", [])
        bc_ready = cDV.n
        for kd in range(11):
            last = (kd == 10)
            def xna2(e, kd=kd, tt=tt, last=last):
                xt = chunk_col(B, kd, tt)
                e.tensor_tensor(xt, xt, E_mbc(tt), ALU.subtract)
                ins = e.tensor_tensor(xt, xt, E_rbc(tt), ALU.mult)
                if last:
                    ins.then_inc(cDV.sem, 1)
            v_ops.append(xna2)
        cDV.n += 1
        g_ops2.append(lambda e, v=bc_ready: e.wait_ge(cDV.sem, v))
        for kd in range(11, NKD):
            last = (kd == NKD - 1)
            def xna2p(e, kd=kd, tt=tt, last=last):
                xt = chunk_col(B, kd, tt)
                e.tensor_tensor(xt, xt, E_mbc(tt), ALU.subtract)
                ins = e.tensor_tensor(xt, xt, E_rbc(tt), ALU.mult)
                if last:
                    ins.then_inc(cPL.sem, 1)
            g_ops2.append(xna2p)
        cPL.n += 1
        ln2col[tt] = [(cDV, cDV.n), (cPL, cPL.n)]

    cur = {}
    def consts(e):
        e.dma_start(masksS[:], masks_in).then_inc(cH.sem, 16)
        e.dma_start(identS[:], ident_in).then_inc(cH.sem, 16)
        e.dma_start(identF[:], identf_in).then_inc(cH.sem, 16)
        e.dma_start(onescS[:], onesc_in).then_inc(cH.sem, 16)
        e.dma_start(onesr, onesr_in).then_inc(cH.sem, 16)
    cur["sync"] = [consts]
    cH.n += 80
    def wg_load(e):
        e.memset(zeroS[:], 0.0)
        e.dma_start(wgateS[:], wgate_in).then_inc(cG.sem, 16)
    cur["gpsimd"] = [wg_load]
    cG.n += 16
    end_waits(cur)
    run_block(cur)

    wkb_done = [0]
    try:
     for s in range(NSEQ):
        cur = {}
        xh_tgt = {}
        if s > 0:
            def mreload(e):
                e.dma_start(masksS[:], masks_in).then_inc(cH.sem, 16)
            cur.setdefault("sync", []).append(mreload)
            cH.n += 16
        g_ops = cur.setdefault("gpsimd", [])
        for tt in range(NCOL):
            def xhat_load(e, s=s, tt=tt):
                e.dma_start(A[:, tt * 8192:(tt + 1) * 8192]
                            .rearrange("p (k t) -> p k t", k=NKD),
                            xb_in[s][:, tt]).then_inc(cG.sem, 16)
            g_ops.append(xhat_load)
            cG.n += 16
            xh_tgt[tt] = cG.n
        ws_q = {"j": 0, "consumed": {}, "dma_done": {}}
        prefetch_w(cur, ws_q, w_in["wq"], [0, 1], C_wbuf, 2)
        if s == 0:
            def wk_cast(e):
                for w8 in range(NW8):
                    e.dma_start(wkb[w8], w_in["wk"][w8]).then_inc(cG.sem, 16)
            cur.setdefault("gpsimd", []).append(wk_cast)
            cG.n += 16 * NW8
            wkb_done[0] = cG.n
        ln1col = {}
        xnst_tgt = {}
        emit_ln_stats(cur, xhat_col, C_eb, ln1_finish, col_wait=xh_tgt)
        if s == 0:
            tap_dma(cur, tap_xhat, A[:, 0:NKD * T], [(cDV, cDV.n)])
        if tap_rows is not None and s == 0:
            def trows(e, v=cDV.n):
                e.wait_ge(cDV.sem, v)
                e.dma_start(tap_rows[1:2, :], R_row).then_inc(cH.sem, 16)
            cur.setdefault("sync", []).append(trows)
            cH.n += 16
        emit_proj(cur, w_in["wq"], list(range(NW8)), C_wbuf,
                  xhat_col, list(range(NCOL)),
                  lambda w8, ocs, tt: chunk_col(B, w8 * 2 + ocs, tt),
                  AF.Silu, rank1=False, rscale=False, rbc_fn=None, eb_fn=None,
                  wstate=ws_q, col_wait=ln1col)
        pq_pe_end = cPE.n
        if s == 0:
            tap_dma(cur, tap_q, B[:, 0:NKD * T], [(cAC, cAC.n)])

        ws_k = {"j": 0, "consumed": {}, "dma_done": {}}
        eb_k = {"free": {}, "rot": 0}
        if s == 0:
            cur.setdefault("sync", []).append(
                lambda e, v=wkb_done[0]: e.wait_ge(cG.sem, v))
        cur.setdefault("sync", []).append(
            lambda e, v=pq_pe_end: e.wait_ge(cPE.sem, v))
        xq_guard = {}
        xq_tgt = {}
        def emit_xq(kq):
            g_ops = cur.setdefault("gpsimd", [])
            if kq >= 2:
                need = xq_guard[kq - 2]
                g_ops.append(lambda e, v=need: e.wait_ge(cPE.sem, v))
            else:
                g_ops.append(lambda e, v=pq_pe_end: e.wait_ge(cPE.sem, v))
            def xq_load(e, kq=kq, v=xnst_tgt[kq]):
                e.wait_ge(cH.sem, v)
                e.dma_start(A_xq(kq % 2).rearrange("p (k t) -> p k t", k=NKD),
                            xnscr[:, kq]).then_inc(cG.sem, 16)
            g_ops.append(xq_load)
            cG.n += 16
            xq_tgt[kq] = cG.n
        emit_xq(0)
        emit_xq(1)
        for kq in range(NCOL):
            cur.setdefault("tensor", []).append(
                lambda e, v=xq_tgt[kq]: e.wait_ge(cG.sem, v))
            xqv = A_xq(kq % 2).rearrange("p (k t) -> p k t", k=NKD)
            emit_proj(cur, wkb, list(range(NW8)), A_wbuf,
                      lambda kd, tt, xqv=xqv: xqv[:, kd, :], [kq],
                      lambda w8, ocs, tt: A_kq_tile(w8 * 2 + ocs),
                      AF.Silu, rank1=False, rscale=False, rbc_fn=None, eb_fn=None,
                      wstate=ws_k, eb_state=eb_k, wqueue="sync", wctr=cH)
            xq_guard[kq] = cPE.n
            if kq + 2 < NCOL:
                emit_xq(kq + 2)
            k_done = cAC.n
            if kq == 0 and tap_k is not None and s == 0:
                tap_dma(cur, tap_k, A[:, 0:NKD * CW], [(cAC, k_done)])

            t_ops = cur.setdefault("tensor", [])
            v_ops = cur.setdefault("vector", [])
            sc_ops = cur.setdefault("scalar", [])
            t_ops.append(lambda e, v=k_done: e.wait_ge(cAC.sem, v))
            roti = [0]
            for kcl in range(4):
                kc = kq * 4 + kcl
                for qq in range(kq, 4):
                    pm_i = roti[0]
                    roti[0] = (roti[0] + 1) % 3
                    pm_wait(t_ops, pm_i)
                    pm = PM[pm_i]
                    def smm(e, pm=pm, kcl=kcl, qq=qq):
                        ins = None
                        for od in range(NKD):
                            ins = e.matmul(pm[:],
                                           A_kq_tile(od)[:, kcl * 128:(kcl + 1) * 128],
                                           chunk_col(B, od, qq),
                                           start=(od == 0), stop=(od == NKD - 1))
                        ins.then_inc(cPE.sem, 1)
                    t_ops.append(smm)
                    cPE.n += 1
                    pev = cPE.n
                    dst = C_stile(s_off[(kc, qq)])
                    if qq == kq:
                        def sev(e, pm=pm, dst=dst, kcl=kcl, v=pev):
                            e.wait_ge(cPE.sem, v)
                            e.tensor_tensor(dst, pm[:], mask_ap(kcl), ALU.mult).then_inc(cDV.sem, 1)
                        v_ops.append(sev)
                        cDV.n += 1
                        pm_state[pm_i] = (cDV, cDV.n)
                    else:
                        def sev2(e, pm=pm, dst=dst, v=pev):
                            e.wait_ge(cPE.sem, v)
                            e.copy(dst, pm[:]).then_inc(cAC.sem, 1)
                        sc_ops.append(sev2)
                        cAC.n += 1
                        pm_state[pm_i] = (cAC, cAC.n)
            xq_guard[kq] = cPE.n
        end_waits(cur)
        run_block(cur)
        check_stop('PKS')

        if tap_s is not None and s == 0:
            cur = {}
            tap_dma(cur, tap_s, C[:, 0:40 * CW])
            end_waits(cur)
            run_block(cur)

        ws_vg = {"j": 0, "consumed": {}, "dma_done": {}}
        eb_vg = {"free": {}, "rot": 0}
        o2t_box = [0]
        xh2_tgt = {}
        cur = {}
        eb2_free = {}
        po_prev = [0]
        pvt_prev = [0]
        for eq in range(4):
            if eq == 0:
                s_ops0 = cur.setdefault("sync", [])
                xh2_tgt[2] = (cH, 0)
                xh2_tgt[3] = (cH, 0)
                for tt in range(2):
                    def xhat_load2(e, tt=tt, v=xnst_tgt[tt]):
                        e.wait_ge(cH.sem, v)
                        e.dma_start(A[:, tt * 8192:(tt + 1) * 8192]
                                    .rearrange("p (k t) -> p k t", k=NKD),
                                    xnscr[:, tt]).then_inc(cH.sem, 16)
                    s_ops0.append(xhat_load2)
                    cH.n += 16
                    xh2_tgt[tt] = (cH, cH.n)
            t_ops = cur.setdefault("tensor", [])
            v_ops = cur.setdefault("vector", [])
            sc_ops = cur.setdefault("scalar", [])
            s_ops = cur.setdefault("sync", [])
            if eq > 0:
                sc_ops.append(lambda e, v=po_prev[0]: e.wait_ge(cPE.sem, v))
            if s == 0 and eq == 0:
                pass
            roti_v = [0]
            for w8l in range(2):
                w8 = 2 * eq + w8l
                jv = ws_vg["j"]
                if jv >= 2:
                    need = ws_vg["consumed"][jv - 2]
                    cur.setdefault("gpsimd", []).append(
                        lambda e, v=need: e.wait_ge(cPE.sem, v))
                dst = B_wbuf(jv % 2)[:].rearrange("p (k o) -> p k o", k=16)
                def fw(e, dst=dst, w8=w8):
                    e.dma_start(dst, w_in["wv"][w8]).then_inc(cG.sem, 16)
                cur.setdefault("gpsimd", []).append(fw)
                cG.n += 16
                ws_vg["dma_done"][jv] = cG.n
                wv_view = B_wbuf(jv % 2)[:].rearrange("p (k o) -> p k o", k=16)
                t_ops.append(lambda e, v=cG.n: e.wait_ge(cG.sem, v))
                for tc in range(16):
                    if eq == 0 and w8l == 0:
                        c, v = xh2_tgt[tc // 4]
                        t_ops.append(lambda e, c=c, v=v: e.wait_ge(c.sem, v))
                    pm_i = roti_v[0]
                    roti_v[0] = (roti_v[0] + 1) % 3
                    pm_wait(t_ops, pm_i)
                    pm = PM[pm_i]
                    xsl = A[:, (tc // 4) * 8192 + (tc % 4) * 128:
                            (tc // 4) * 8192 + (tc % 4) * 128 + 1]
                    def vgrp(e, pm=pm, tc=tc, wv_view=wv_view):
                        ins = None
                        for kd in range(NKD):
                            xs = A[:, (tc // 4) * 8192 + kd * CW + (tc % 4) * 128:
                                   (tc // 4) * 8192 + kd * CW + (tc % 4) * 128 + 128]
                            ins = e.matmul(pm[:, 0:256], xs, wv_view[:, kd, :],
                                           start=(kd == 0), stop=(kd == NKD - 1))
                        ins.then_inc(cPE.sem, 1)
                    t_ops.append(vgrp)
                    cPE.n += 1
                    pev = cPE.n
                    def vev(e, pm=pm, tc=tc, w8l=w8l, v=pev):
                        e.wait_ge(cPE.sem, v)
                        e.activation(B_vt(tc)[:, w8l * 256:(w8l + 1) * 256],
                                     pm[:, 0:256],
                                     AF.Silu).then_inc(cAC.sem, 1)
                    sc_ops.append(vev)
                    cAC.n += 1
                    pm_state[pm_i] = (cAC, cAC.n)
                ws_vg["consumed"][jv] = cPE.n
                ws_vg["j"] = jv + 1
                ws_vg["issued"] = jv + 1
            pvt_done = cAC.n
            pvt_prev[0] = pvt_done
            if eq == 0 and s == 0:
                tap_dma(cur, tap_vt, B[:, 0:NKD * CW], [(cH, pvt_done)])
            def g_dest(w8, ocs, tt, eq=eq):
                oc4 = (w8 - 2 * eq) * 2 + ocs
                return B_g(oc4)[:, tt * CW:(tt + 1) * CW]
            emit_proj(cur, w_in["wg"], [2 * eq, 2 * eq + 1], B_wbuf,
                      xhat_col, list(range(NCOL)),
                      g_dest, AF.Sigmoid, rank1=False, rscale=False,
                      rbc_fn=None, eb_fn=None, wstate=ws_vg, eb_state=eb_vg)
            pg_done = cAC.n
            pg_pe_done = cPE.n
            if eq == 0 and s == 0:
                tap_dma(cur, tap_g, B[:, 8192:8192 + 4 * T], [(cAC, pg_done)])
            if eq == 3:
                def o2_load(e, v=pg_pe_done):
                    e.wait_ge(cPE.sem, v)
                    e.dma_start(A[:, 0:24576].rearrange("p (k t) -> p k t", k=12),
                                o2scr[:, 0:12, :]).then_inc(cH.sem, 16)
                s_ops.append(o2_load)
                cH.n += 16
            t_ops = cur.setdefault("tensor", [])
            v_ops = cur.setdefault("vector", [])
            t_ops.append(lambda e, v=pvt_done: e.wait_ge(cAC.sem, v))
            v_ops.append(lambda e, v=pg_done: e.wait_ge(cAC.sem, v))
            roti = [0]
            eb_free = eb2_free
            for qq in range(NCOL):
                for oc4 in range(4):
                    pm_i = roti[0]
                    roti[0] = (roti[0] + 1) % 3
                    pm_wait(t_ops, pm_i)
                    pm = PM[pm_i]
                    nkc = 4 * qq + 4
                    def omm(e, pm=pm, qq=qq, oc4=oc4, nkc=nkc):
                        ins = None
                        for kc in range(nkc):
                            ins = e.matmul(pm[:], B_vt(kc)[:, oc4 * 128:(oc4 + 1) * 128],
                                           C_stile(s_off[(kc, qq)]),
                                           start=(kc == 0), stop=(kc == nkc - 1))
                        ins.then_inc(cPE.sem, 1)
                    t_ops.append(omm)
                    cPE.n += 1
                    pev = cPE.n
                    ebs = (qq * 4 + oc4) % 4
                    if ebs in eb_free:
                        v_ops.append(lambda e, v=eb_free[ebs]: e.wait_ge(cH.sem, v))
                    def oev(e, pm=pm, qq=qq, oc4=oc4, ebs=ebs, v=pev):
                        e.wait_ge(cPE.sem, v)
                        e.tensor_tensor(B_eb2(ebs), pm[:],
                                        B_g(oc4)[:, qq * CW:(qq + 1) * CW],
                                        ALU.mult).then_inc(cDV.sem, 1)
                    v_ops.append(oev)
                    cDV.n += 1
                    pm_state[pm_i] = (cDV, cDV.n)
                    dvv = cDV.n
                    oc = eq * 4 + oc4
                    def ost(e, ebs=ebs, oc=oc, qq=qq, v=dvv):
                        e.wait_ge(cDV.sem, v)
                        e.dma_start(o2scr[:, oc, qq * CW:(qq + 1) * CW],
                                    B_eb2(ebs)).then_inc(cH.sem, 16)
                    s_ops.append(ost)
                    cH.n += 16
                    eb_free[ebs] = cH.n
            po_prev[0] = cPE.n
            if eq == 3:
                def o2_load2(e, v=cH.n):
                    e.wait_ge(cH.sem, v)
                    e.dma_start(A[:, 24576:32768].rearrange("p (k t) -> p k t", k=4),
                                o2scr[:, 12:16, :]).then_inc(cH.sem, 16)
                s_ops.append(o2_load2)
                cH.n += 16
                o2t_box[0] = cH.n
        po_end = cPE.n

        g_ops = cur.setdefault("gpsimd", [])
        t_ops = cur.setdefault("tensor", [])
        v_ops = cur.setdefault("vector", [])
        sc_ops = cur.setdefault("scalar", [])
        s_ops = cur.setdefault("sync", [])
        g_ops.append(lambda e, v=po_end: e.wait_ge(cPE.sem, v))
        s_ops.append(lambda e, v=po_end: e.wait_ge(cPE.sem, v))
        t_ops.append(lambda e, v=o2t_box[0]: e.wait_ge(cH.sem, v))

        order = [(w8, ocs, tt) for w8 in range(NW8) for ocs in range(2)
                 for tt in range(NCOL)]
        xr_tgt = {}
        xr_free = {}
        def xr_dma(i):
            w8, ocs, tt = order[i]
            oc = w8 * 2 + ocs
            j = i % 4
            if j in xr_free:
                c, v = xr_free[j]
                s_ops.append(lambda e, c=c, v=v: e.wait_ge(c.sem, v))
            def f(e, oc=oc, tt=tt, j=j, s=s):
                e.dma_start(C_xr(j), xb_in[s][:, tt, oc, :]).then_inc(cH.sem, 16)
            s_ops.append(f)
            cH.n += 16
            xr_tgt[i] = cH.n
        for i in range(3):
            xr_dma(i)

        dma_done = {}
        consumed = {}
        def wdma_b5(j):
            if j >= 2:
                need = consumed[j - 2]
                g_ops.append(lambda e, v=need: e.wait_ge(cPE.sem, v))
            dst = C_wbuf(j % 2)[:].rearrange("p (k o) -> p k o", k=16)
            def f(e, dst=dst, j=j):
                e.dma_start(dst, w_in["wp"][j]).then_inc(cG.sem, 16)
            g_ops.append(f)
            cG.n += 16
            dma_done[j] = cG.n
        for j in range(2):
            wdma_b5(j)

        ev_free = {}
        roti = [0]
        i_lin = 0
        for j in range(NW8):
            if j >= 2:
                wdma_b5(j)
            t_ops.append(lambda e, v=dma_done[j]: e.wait_ge(cG.sem, v))
            wv = C_wbuf(j % 2)[:].rearrange("p (k o) -> p k o", k=16)
            for ocs in range(2):
                for tt in range(NCOL):
                    pm_i = roti[0]
                    roti[0] = (roti[0] + 1) % 3
                    pm_wait(t_ops, pm_i)
                    pm = PM[pm_i]
                    def grp(e, pm=pm, wv=wv, ocs=ocs, tt=tt):
                        ins = None
                        for kd in range(NKD):
                            ins = e.matmul(pm[:], wv[:, kd, ocs * 128:(ocs + 1) * 128],
                                           chunk_col(A, kd, tt),
                                           start=(kd == 0), stop=(kd == NKD - 1))
                        ins.then_inc(cPE.sem, 1)
                    t_ops.append(grp)
                    cPE.n += 1
                    pev = cPE.n
                    w8_, ocs_, tt_ = order[i_lin]
                    oc = w8_ * 2 + ocs_
                    jj = i_lin % 4
                    if i_lin + 3 < len(order):
                        xr_dma(i_lin + 3)
                    if jj in ev_free:
                        for c, v in ev_free[jj]:
                            v_ops.append(lambda e, c=c, v=v: e.wait_ge(c.sem, v))
                    def ad(e, pm=pm, jj=jj, v=pev, hx=xr_tgt[i_lin]):
                        e.wait_ge(cPE.sem, v)
                        e.wait_ge(cH.sem, hx)
                        e.tensor_tensor(C_ev(jj), pm[:], C_xr(jj), ALU.add).then_inc(cDV.sem, 1)
                    v_ops.append(ad)
                    cDV.n += 1
                    dvv = cDV.n
                    pm_state[pm_i] = (cDV, dvv)
                    xr_free[jj] = (cDV, dvv)
                    def st(e, oc=oc, tt=tt_, jj=jj, v=dvv):
                        e.wait_ge(cDV.sem, v)
                        e.dma_start(x2scr[:, oc, tt * CW:(tt + 1) * CW],
                                    C_ev(jj)).then_inc(cH.sem, 16)
                    s_ops.append(st)
                    cH.n += 16
                    h_v = cH.n
                    def cv(e, oc=oc, tt=tt_, jj=jj, v=dvv):
                        e.wait_ge(cDV.sem, v)
                        e.copy(chunk_col(B, oc, tt), C_ev(jj)).then_inc(cAC.sem, 1)
                    sc_ops.append(cv)
                    cAC.n += 1
                    ev_free[jj] = [(cH, h_v), (cAC, cAC.n)]
                    i_lin += 1
            consumed[j] = cPE.n
        end_waits(cur)
        run_block(cur)
        check_stop('B5')

        if tap_x2 is not None and s == 0:
            cur = {}
            tap_dma(cur, tap_x2, x2scr.rearrange("p k t -> p (k t)"))
            end_waits(cur)
            run_block(cur)

        cur = {}
        ws_up = {"j": 0, "consumed": {}, "dma_done": {}}
        prefetch_w(cur, ws_up, w_in["wup"], [0, 1], C_wbuf, 2)
        g_ops = cur.setdefault("gpsimd", [])
        def ffn_pf0(e):
            e.dma_start(C_wbuf(2)[:].rearrange("p (k o) -> p k o", k=16),
                        w_in["wdn"][0]).then_inc(cG.sem, 16)
            e.dma_start(C_wbuf(3)[:].rearrange("p (k o) -> p k o", k=16),
                        w_in["wex"][0]).then_inc(cG.sem, 16)
        g_ops.append(ffn_pf0)
        cG.n += 32
        ffn_pf = cG.n
        t_ops = cur.setdefault("tensor", [])
        v_ops = cur.setdefault("vector", [])
        sc_ops = cur.setdefault("scalar", [])
        s_ops = cur.setdefault("sync", [])
        ln2col = {}
        emit_ln_stats(cur, lambda kd, tt: chunk_col(B, kd, tt), C_eb5, ln2_finish)
        wgv = wgateS[:].rearrange("p (k o) -> p k o", o=8)
        for tt in range(NCOL):
            for c, v in ln2col[tt]:
                t_ops.append(lambda e, c=c, v=v: e.wait_ge(c.sem, v))
            pm_wait(t_ops, 3)
            def lmm(e, tt=tt):
                ins = None
                for kd in range(NKD):
                    ins = e.matmul(PM[3][0:8, :], wgv[:, kd, :],
                                   chunk_col(B, kd, tt),
                                   start=(kd == 0), stop=(kd == NKD - 1))
                ins.then_inc(cPE.sem, 1)
            t_ops.append(lmm)
            cPE.n += 1
            pev = cPE.n
            def lcp(e, tt=tt, v=pev):
                e.wait_ge(cPE.sem, v)
                e.tensor_copy(Lsb, PM[3][0:8, :]).then_inc(cDV.sem, 1)
            v_ops.append(lcp)
            cDV.n += 1
            pm_state[3] = (cDV, cDV.n)
            acv = cDV.n
            t_ops.append(lambda e, v=acv: e.wait_ge(cDV.sem, v))
            for jj in range(4):
                pt_i = jj % 2
                def ltr(e, jj=jj, pt_i=pt_i):
                    e.matmul(PT[pt_i][0:1, 0:2].bitcast(F32), zeroS[:, 0:1],
                             zeroS[0:128, 0:1], start=True, stop=True)
                    e.transpose(PT[pt_i][:, 0:8], Lsb[:, jj * 128:(jj + 1) * 128],
                                identS[0:8, 0:8])
                    e.matmul(PS0[0:1, 0:1], zeroS[:, 0:1], zeroS[0:128, 0:1],
                             start=True, stop=True).then_inc(cPE.sem, 1)
                t_ops.append(ltr)
                cPE.n += 1
                pv2 = cPE.n
                g = tt * 4 + jj
                def lto(e, g=g, pt_i=pt_i, v=pv2):
                    e.wait_ge(cPE.sem, v)
                    e.copy(Ltok[:, g * 8:(g + 1) * 8],
                           PT[pt_i][:, 0:8]).then_inc(cAC.sem, 1)
                sc_ops.append(lto)
                cAC.n += 1
                t_ops.append(lambda e, v=cAC.n: e.wait_ge(cAC.sem, v))
        lt_done = cAC.n
        emit_proj(cur, w_in["wup"], list(range(NW8)), C_wbuf,
                  lambda kd, tt: chunk_col(B, kd, tt), list(range(NCOL)),
                  lambda w8, ocs, tt: chunk_col(A, w8 * 2 + ocs, tt),
                  AF.Relu, rank1=False, rscale=False, rbc_fn=None, eb_fn=None,
                  wstate=ws_up, col_wait=ln2col)
        Lv = Ltok[:].rearrange("p (g e) -> p g e", e=8)
        n0 = cDV.n
        def top2(e, v=lt_done, n0=n0):
            e.wait_ge(cAC.sem, v)
            e.reduce_max(m1_s, Lv, AX.X).then_inc(cDV.sem, 1)
            e.wait_ge(cDV.sem, n0 + 1)
            e.tensor_tensor(T1_s[:].rearrange("p (g e) -> p g e", e=8), Lv,
                            m1_s[:].rearrange("p g -> p g ()").broadcast_to([128, 16, 8]),
                            ALU.subtract).then_inc(cDV.sem, 1)
            e.wait_ge(cDV.sem, n0 + 2)
            e.tensor_single_scalar(T2_s, T1_s, 0.0, ALU.is_ge).then_inc(cDV.sem, 1)
            e.wait_ge(cDV.sem, n0 + 3)
            e.scalar_tensor_tensor(T2_s, T2_s, -1e30, T1_s, ALU.mult,
                                   ALU.add).then_inc(cDV.sem, 1)
            e.wait_ge(cDV.sem, n0 + 4)
            e.reduce_max(m2_s, T2_s[:].rearrange("p (g e) -> p g e", e=8),
                         AX.X).then_inc(cDV.sem, 1)
        v_ops.append(top2)
        cDV.n += 5
        def t2e(e, v=cDV.n):
            e.wait_ge(cDV.sem, v)
            e.activation(m2_s, m2_s, AF.Exp).then_inc(cAC.sem, 1)
        sc_ops.append(t2e)
        cAC.n += 1
        def t2f(e, v=cAC.n, w1=cDV.n + 1, w2=cDV.n + 2):
            e.wait_ge(cAC.sem, v)
            e.tensor_scalar_add(s_s, m2_s, 1.0).then_inc(cDV.sem, 1)
            e.wait_ge(cDV.sem, w1)
            e.reciprocal(rinv_s, s_s).then_inc(cDV.sem, 1)
            e.wait_ge(cDV.sem, w2)
            e.tensor_tensor(RWS_s, s_s, rinv_s, ALU.mult).then_inc(cDV.sem, 1)
        v_ops.append(t2f)
        cDV.n += 3
        pm_wait(t_ops, 3)
        def rwt(e, v=cDV.n):
            e.wait_ge(cDV.sem, v)
            e.matmul(PM[3][0:1, 0:1], zeroS[:, 0:1], zeroS[0:128, 0:1],
                     start=True, stop=True)
            e.transpose(PM[3][0:16, 0:128], RWS_s, identF[:])
            e.matmul(PS0[0:1, 0:1], zeroS[:, 0:1], zeroS[0:128, 0:1],
                     start=True, stop=True).then_inc(cPE.sem, 1)
        t_ops.append(rwt)
        cPE.n += 1
        def rwc(e, v=cPE.n):
            e.wait_ge(cPE.sem, v)
            e.tensor_copy(row16, PM[3][0:16, 0:128]).then_inc(cDV.sem, 1)
        v_ops.append(rwc)
        cDV.n += 1
        pm_state[3] = (cDV, cDV.n)
        t_ops.append(lambda e, v=cDV.n: e.wait_ge(cDV.sem, v))
        def rwb(e, v=cDV.n):
            e.wait_ge(cDV.sem, v)
            e.dma_start(rwscr.rearrange("(g p) -> g p", g=16), row16).then_inc(cH.sem, 16)
        s_ops.append(rwb)
        cH.n += 16
        def rwb2(e, v=cH.n):
            e.wait_ge(cH.sem, v)
            e.dma_start(rw_row, rwscr.rearrange("(o t) -> o t", o=1)).then_inc(cH.sem, 16)
        s_ops.append(rwb2)
        cH.n += 16
        rw_tgt = cH.n
        if s == 0:
            tap_dma(cur, tap_rws, rwscr, [(cH, rw_tgt)])
            tap_dma(cur, tap_small, SMALL[:], [(cH, rw_tgt)])
            tap_dma(cur, tap_up, A[:, 0:NKD * T], [(cAC, cAC.n)])
        t_ops.append(lambda e, v=rw_tgt: e.wait_ge(cH.sem, v))
        build_rbc(cur, E_rwbc, rw_row)
        up_ac_end = cAC.n
        up_dv_end = cDV.n

        g_ops = cur.setdefault("gpsimd", [])
        t_ops = cur.setdefault("tensor", [])
        v_ops = cur.setdefault("vector", [])
        s_ops = cur.setdefault("sync", [])
        t_ops.append(lambda e, v=up_ac_end: e.wait_ge(cAC.sem, v))
        s_ops.append(lambda e, v=up_ac_end: e.wait_ge(cAC.sem, v))
        s_ops.append(lambda e, v=up_dv_end: e.wait_ge(cDV.sem, v))
        dma_done = {0: ffn_pf}
        consumed = {}
        up_last = ws_up["consumed"][NW8 - 1]
        def wdma_de(j):
            if j >= 2:
                need = consumed[j - 2]
                g_ops.append(lambda e, v=need: e.wait_ge(cPE.sem, v))
            elif j == 1:
                g_ops.append(lambda e, v=up_last: e.wait_ge(cPE.sem, v))
            base = 2 - (j % 2) * 2
            dd = C_wbuf(base)[:].rearrange("p (k o) -> p k o", k=16)
            de = C_wbuf(base + 1)[:].rearrange("p (k o) -> p k o", k=16)
            def f(e, dd=dd, de=de, j=j):
                e.dma_start(dd, w_in["wdn"][j]).then_inc(cG.sem, 16)
                e.dma_start(de, w_in["wex"][j]).then_inc(cG.sem, 16)
            g_ops.append(f)
            cG.n += 32
            dma_done[j] = cG.n
        wdma_de(1)

        order = [(w8, ocs, tt) for w8 in range(NW8) for ocs in range(2)
                 for tt in range(NCOL)]
        xr_tgt = {}
        xr_free = {}
        def x2_dma(i):
            w8, ocs, tt = order[i]
            oc = w8 * 2 + ocs
            j = i % 4
            if j in xr_free:
                c, v = xr_free[j]
                s_ops.append(lambda e, c=c, v=v: e.wait_ge(c.sem, v))
            def f(e, oc=oc, tt=tt, j=j):
                e.dma_start(E_xr(j), x2scr[:, oc, tt * CW:(tt + 1) * CW]).then_inc(cH.sem, 16)
            s_ops.append(f)
            cH.n += 16
            xr_tgt[i] = cH.n
        for i in range(3):
            x2_dma(i)

        ev_free = {}
        i_lin = 0
        for j in range(NW8):
            if j >= 2:
                wdma_de(j)
            t_ops.append(lambda e, v=dma_done[j]: e.wait_ge(cG.sem, v))
            base_j = 2 - (j % 2) * 2
            wdv = C_wbuf(base_j)[:].rearrange("p (k o) -> p k o", k=16)
            wev = C_wbuf(base_j + 1)[:].rearrange("p (k o) -> p k o", k=16)
            for ocs in range(2):
                for tt in range(NCOL):
                    pa_i = i_lin % 2
                    pb_i = 2 + (i_lin % 2)
                    pm_wait(t_ops, pa_i)
                    pm_wait(t_ops, pb_i)
                    pmA, pmB = PM[pa_i], PM[pb_i]
                    def grp(e, pmA=pmA, pmB=pmB, wdv=wdv, wev=wev, ocs=ocs, tt=tt,
                            jw=j):
                        for kd in range(NKD):
                            e.matmul(pmA[:], wdv[:, kd, ocs * 128:(ocs + 1) * 128],
                                     chunk_col(A, kd, tt),
                                     start=(kd == 0), stop=(kd == NKD - 1))
                        ins = None
                        for kd in range(NKD):
                            ins = e.matmul(pmB[:],
                                           wev[:, kd, ocs * 128:(ocs + 1) * 128],
                                           chunk_col(B, kd, tt),
                                           start=(kd == 0),
                                           stop=(kd == NKD - 1))
                        ins.then_inc(cPE.sem, 1)
                    t_ops.append(grp)
                    cPE.n += 1
                    pev = cPE.n
                    oc = order[i_lin][0] * 2 + ocs
                    jj = i_lin % 4
                    if i_lin + 3 < len(order):
                        x2_dma(i_lin + 3)
                    if jj in ev_free:
                        c, v = ev_free[jj]
                        v_ops.append(lambda e, c=c, v=v: e.wait_ge(c.sem, v))
                    def comb(e, pmA=pmA, pmB=pmB, tt=tt, jj=jj, v=pev,
                             hx=xr_tgt[i_lin]):
                        e.wait_ge(cPE.sem, v)
                        e.wait_ge(cH.sem, hx)
                        e.tensor_tensor(C_ev2(jj), pmB[:], E_rwbc(tt), ALU.mult)
                        e.tensor_tensor(C_ev2(jj), pmA[:], C_ev2(jj), ALU.add)
                        e.tensor_tensor(C_ev2(jj), E_xr(jj), C_ev2(jj),
                                        ALU.add).then_inc(cDV.sem, 1)
                    v_ops.append(comb)
                    cDV.n += 1
                    dvv = cDV.n
                    pm_state[pa_i] = (cDV, dvv)
                    pm_state[pb_i] = (cDV, dvv)
                    xr_free[jj] = (cDV, dvv)
                    def ostt(e, oc=oc, tt=tt, jj=jj, v=dvv, s=s):
                        e.wait_ge(cDV.sem, v)
                        e.dma_start(out_t[s][oc * 128:(oc + 1) * 128,
                                             tt * CW:(tt + 1) * CW],
                                    C_ev2(jj)).then_inc(cH.sem, 16)
                    s_ops.append(ostt)
                    cH.n += 16
                    ev_free[jj] = (cH, cH.n)
                    i_lin += 1
            consumed[j] = cPE.n
        end_waits(cur)
        run_block(cur)
        check_stop('SEQ' + str(s))
    except _StopEmit:
        pass

    ctx.close()
    return nc


def _block_weight(w):
    return np.ascontiguousarray(w.reshape(16, 128, 8, 256).transpose(2, 1, 0, 3))


def _make_consts():
    masks = np.zeros((128, 2048), dtype=ml_dtypes.bfloat16)
    jk = np.arange(128)[:, None]
    iq = np.arange(512)[None, :]
    for j in range(4):
        masks[:, j * 512:(j + 1) * 512] = (jk <= iq - 128 * j).astype(ml_dtypes.bfloat16)
    ident = np.eye(128, dtype=ml_dtypes.bfloat16)
    identf = np.eye(128, dtype=np.float32)
    onesc = np.ones((128, 1), dtype=ml_dtypes.bfloat16)
    onesr = np.ones((1, 128), dtype=np.float32)
    return masks, ident, identf, onesc, onesr


def kernel(L0_attn_x, L0_attn_Wq, L0_attn_Wk, L0_attn_Wv, L0_attn_Wg, L0_attn_Wp,
           L0_ffn_W_up, L0_ffn_W_down, L0_ffn_W_gate, L0_ffn_W_expert):
    x = np.asarray(L0_attn_x, dtype=np.float32)
    wq = _block_weight(np.ascontiguousarray(np.asarray(L0_attn_Wq, np.float32).T))
    wk = _block_weight(np.ascontiguousarray(np.asarray(L0_attn_Wk, np.float32).T))
    wv = _block_weight(np.ascontiguousarray(np.asarray(L0_attn_Wv, np.float32).T))
    wg = _block_weight(np.ascontiguousarray(np.asarray(L0_attn_Wg, np.float32).T))
    wp = _block_weight(np.ascontiguousarray(np.asarray(L0_attn_Wp, np.float32).T))
    wup = _block_weight(np.ascontiguousarray(np.asarray(L0_ffn_W_up, np.float32).T))
    wdn = _block_weight(np.ascontiguousarray(np.asarray(L0_ffn_W_down, np.float32).T))
    wex = _block_weight(np.ascontiguousarray(np.asarray(L0_ffn_W_expert, np.float32)))
    wgt = np.ascontiguousarray(np.asarray(L0_ffn_W_gate, np.float32).T
                               .reshape(16, 128, 8).transpose(1, 0, 2).reshape(128, 128))
    masks, ident, identf, onesc, onesr = _make_consts()

    nc = build_nc()
    in_maps = []
    for c in range(8):
        xs = x[2 * c:2 * c + 2]
        xbl = np.ascontiguousarray(
            xs.transpose(0, 2, 1).reshape(NSEQ, 16, 128, NCOL, CW)
            .transpose(0, 2, 3, 1, 4))
        in_maps.append(dict(xb=xbl, wq=wq, wk=wk, wv=wv, wg=wg, wp=wp,
                            wup=wup, wdn=wdn, wex=wex, wgate=wgt,
                            masks=masks, ident=ident, identf=identf,
                            onesc=onesc, onesr=onesr))
    res = run_bass_kernel_spmd(nc, in_maps, core_ids=list(range(8)))
    out = np.empty((16, T, D), dtype=np.float32)
    for c in range(8):
        ot = np.asarray(res.results[c]["outT"])
        out[2 * c] = ot[0].T
        out[2 * c + 1] = ot[1].T
    return out
